# version 38
# baseline (speedup 1.0000x reference)
import contextlib
import numpy as np
import concourse.bass as bass
import concourse.mybir as mybir
from concourse.bass_utils import run_bass_kernel_spmd

F32 = mybir.dt.float32
BF16 = mybir.dt.bfloat16
AF = mybir.ActivationFunctionType
ALU = mybir.AluOpType

P = 128
D = 1024
NDT = 8
NST = 16
W = 128
TRE = 2048
TTOT = TRE + W
NPH = 3
TP = 768
NSUB = TP // P
TPL = (640, 768, 768)
NSUBL = (5, 6, 6)
SUBBASE = (0, 5, 11)
PSTART = (0, 640, 1408)
PI_HALF = 1.5707963267948966
ENGS = ("sync", "vector", "scalar", "tensor", "gpsimd")
SEMS = ("LI", "LA", "LB", "LC", "LD", "LE", "LF",
        "LX0", "LX1", "LX2", "LX3", "LX4", "LX5", "LR", "LR1", "LAC", "O0", "O1",
        "LW0", "LW1", "V", "S", "T", "G")


def _build():
    nc = bass.Bass("TRN2", target_bir_lowering=False, debug=False)

    def din(name, shape):
        return nc.dram_tensor(name, shape, F32, kind="ExternalInput").ap()

    xin = din("xin", [TTOT, D])
    wdt = din("wdt", [D, D])
    wot = din("wot", [D, D])
    tht = din("tht", [D, D])
    thn = din("thn", [D, D])
    pht = din("pht", [D, D])
    phn = din("phn", [D, D])
    cst = din("cst", [P, 272])
    acl = din("acl", [P, P])
    out = nc.dram_tensor("out", [TRE, D], F32, kind="ExternalOutput").ap()

    es = contextlib.ExitStack()
    with es:
        def sbuf(n, sh, dt=F32):
            return es.enter_context(nc.sbuf_tensor(n, sh, dt))

        def psum(n):
            return es.enter_context(nc.psum_tensor(n, [P, 512], F32))

        wd_bf = sbuf("wd_bf", [P, 8192], BF16)
        wo_bf = sbuf("wo_bf", [P, 8192], BF16)
        cos_bf = sbuf("cos_bf", [P, 8192], BF16)
        sin_bf = sbuf("sin_bf", [P, 8192], BF16)
        bdg_bf = sbuf("bdg_bf", [P, NDT * NST * P], BF16)
        xn32 = sbuf("xn32", [P, NDT * TP])
        xnbf = sbuf("xnbf", [P, NDT * TP], BF16)
        delta = sbuf("delta", [P, 2 * TP])
        vbuf = sbuf("vbuf", [P, 2 * TP])
        abar = sbuf("abar", [P, 3 * TP])
        grot = sbuf("grot", [P, 4 * TP], BF16)
        ybf = sbuf("ybf", [P, NDT * TP], BF16)
        sfm = sbuf("sfm", [P, 2 * 512])
        xsub = sbuf("xsub", [P, 2 * 1024])
        etmp = sbuf("etmp", [P, 2 * 512])
        r2 = sbuf("r2", [P, 2 * 512])
        i2 = sbuf("i2", [P, 2 * 512])
        osub = sbuf("osub", [P, 2 * 512])
        xres = sbuf("xres", [P, 2 * 512])
        gcar = sbuf("gcar", [P, P], BF16)
        nega = sbuf("nega", [P, P])
        csts = sbuf("csts", [P, 272])
        cvals = sbuf("cvals", [P, 3])
        bn6 = sbuf("bn6", [P, 12 * NSUB])
        bn2 = sbuf("bn2", [P, 2])
        bnA = sbuf("bnA", [P, 2 * NSUB])
        lnv = sbuf("lnv", [P, NSUB])
        rstd = sbuf("rstd", [P, 1])
        negmr = sbuf("negmr", [P, 1])
        rstd6 = sbuf("rstd6", [P, NSUB])
        negmr6 = sbuf("negmr6", [P, NSUB])

        idn_s = csts[:, 0:128]
        bdc_s = csts[:, 128:136]
        boc_s = csts[:, 136:144]
        bcol_s = csts[:, 144:272]

        pt = [psum("pt0"), psum("pt1")]
        pd = [psum("pd0"), psum("pd1")]
        py = [psum("py0"), psum("py1")]
        pe = [psum("pe0"), psum("pe1")]

        semh = {s: es.enter_context(nc.semaphore(s)) for s in SEMS}
        cnt = {s: 0 for s in SEMS}
        wm = {}
        ops = {e: [] for e in ENGS}

        def wait(e, s, v):
            if v is None or v <= 0:
                return
            if wm.get((e, s), 0) >= v:
                return
            wm[(e, s)] = v
            h = semh[s]
            ops[e].append(lambda eng, h=h, v=v: eng.wait_ge(h, v))

        def run(e, fn, inc=None, k=1):
            if inc is None:
                ops[e].append(fn)
                return None
            cnt[inc] += k
            v = cnt[inc]
            h = semh[inc]
            ops[e].append(lambda eng, fn=fn, h=h, k=k: fn(eng).then_inc(h, k))
            return v

        run("sync", lambda e: e.dma_start(out=csts[:], in_=cst), "LI", 16)
        lacl = run("sync", lambda e: e.dma_start(out=nega[:], in_=acl), "LAC", 16)
        run("vector", lambda e: e.memset(gcar[:], 0.0), "V")
        v_ph = run("vector", lambda e: e.memset(cvals[:, 0:1], PI_HALF), "V")
        wait("vector", "V", v_ph)
        run("vector", lambda e: e.tensor_scalar(
            out=cvals[:, 0:1], in0=cvals[:, 0:1],
            scalar1=-4.3711388286737929e-08, scalar2=None, op0=ALU.add), "V")
        run("vector", lambda e: e.memset(cvals[:, 1:2], 1e-5), "V")
        v_cv = run("vector", lambda e: e.memset(cvals[:, 2:3], -1.0), "V")

        wslot = [r2[:, 0:512], r2[:, 512:1024], i2[:, 0:512],
                 i2[:, 512:1024], etmp[:, 0:512], etmp[:, 512:1024]]
        wsem = ["LA", "LB", "LC", "LD", "LE", "LF"]
        wfree = [("G", 0)] * 6
        trig_cmds = []
        trig_add_jobs = []
        wo_cast_jobs = []
        winit_last_g = 0
        winit_vec = [0]
        trig_s_last = [0]
        trig_h1 = [0]
        h0_idx = [kk * 2 for kk in range(8)] + [16 + kk * 2 for kk in range(8)]
        h1_idx = [kk * 2 + 1 for kk in range(8)] + [17 + kk * 2 for kk in range(8)]
        ip = 0

        def wslot_next():
            nonlocal ip
            sl = ip % 6
            ip += 1
            return sl

        p0_xslots = [xsub[:, 0:1024], xsub[:, 1024:2048], delta[:, 0:1024],
                     abar[:, 0:1024], sfm[:, 0:1024], vbuf[:, 0:1024]]
        p0_xsems = ["LX0", "LX1", "LX2", "LX3", "LX4", "LX5"]
        p0_lxev = []
        for ts in range(NSUBL[0]):
            lx = run("sync", lambda e, r=ts * P, ts=ts: e.dma_start(
                out=p0_xslots[ts], in_=xin[r:r + P, :]), p0_xsems[ts], 16)
            p0_lxev.append(lx)
        grp_done = {}
        wds_slots = [vbuf[:, 0:512], vbuf[:, 512:1024], osub[:, 512:1024],
                     xres[:, 0:512], xres[:, 512:1024]]
        wds_sems = ["LX5", "LW0", "LW1", "LR", "LR1"]
        wd_s_jobs = []
        ev_wds_s = [0]
        bdg_ev = [("S", 0)] * NDT

        def emit_diag_s(dtq, nlo, nhi):
            wait("scalar", "LI", 16)
            ev = None
            for nq in range(nlo, nhi):
                t = dtq * NST + nq
                ev = run("scalar", lambda e, t=t: e.activation(
                    out=bdg_bf[:, t * P:(t + 1) * P], in_=idn_s,
                    func=AF.Copy, scale=bcol_s[:, t:t + 1]), "S")
            return ev
        for dtq in range(2):
            bdg_ev[dtq] = ("S", emit_diag_s(dtq, 0, NST))
        jobs = []
        for kk in range(8):
            for h in range(2):
                jobs.append(("cast", wdt[kk * P:(kk + 1) * P, h * 512:(h + 1) * 512],
                             None, wd_bf, kk * 1024 + h * 512))
        for kk in range(8):
            for h in range(2):
                jobs.append(("trig", tht[kk * P:(kk + 1) * P, h * 512:(h + 1) * 512],
                             thn[kk * P:(kk + 1) * P, h * 512:(h + 1) * 512],
                             cos_bf, kk * 1024 + h * 512))
        for kk in range(8):
            for h in range(2):
                jobs.append(("trig", pht[kk * P:(kk + 1) * P, h * 512:(h + 1) * 512],
                             phn[kk * P:(kk + 1) * P, h * 512:(h + 1) * 512],
                             sin_bf, kk * 1024 + h * 512))
        for kk in range(8):
            for h in range(2):
                jobs.append(("wo", wot[kk * P:(kk + 1) * P, h * 512:(h + 1) * 512],
                             None, wo_bf, kk * 1024 + h * 512))

        for kind, src0, src1, dst, off in jobs:
            if kind == "cast":
                piece = off // 512
                if piece >= 11:
                    q = piece - 11
                    la = run("sync", lambda e, s=src0, q=q: e.dma_start(
                        out=wds_slots[q], in_=s), wds_sems[q], 16)
                    wd_s_jobs.append((q, la, dst, off))
                    continue
                sl = wslot_next()
                wait("sync", *wfree[sl])
                la = run("sync", lambda e, s=src0, sl=sl: e.dma_start(
                    out=wslot[sl], in_=s), wsem[sl], 16)
                wait("gpsimd", wsem[sl], la)
                g_ev = run("gpsimd", lambda e, d=dst, o=off, sl=sl: e.tensor_copy(
                    out=d[:, o:o + 512], in_=wslot[sl]), "G")
                wfree[sl] = ("G", g_ev)
                winit_last_g = g_ev
                grp_done[id(dst)] = g_ev
            elif kind == "trig":
                s1_ = wslot_next()
                s2_ = wslot_next()
                wait("sync", *wfree[s1_])
                la = run("sync", lambda e, s=src0, sl=s1_: e.dma_start(
                    out=wslot[sl], in_=s), wsem[s1_], 16)
                wait("sync", *wfree[s2_])
                lb = run("sync", lambda e, s=src1, sl=s2_: e.dma_start(
                    out=wslot[sl], in_=s), wsem[s2_], 16)
                wait("gpsimd", wsem[s1_], la)
                wait("gpsimd", wsem[s2_], lb)
                g_ev = run("gpsimd", lambda e, a=s1_, b=s2_, d=dst, o=off: e.tensor_add(
                    out=d[:, o:o + 512], in0=wslot[a], in1=wslot[b]), "G")
                wfree[s1_] = ("G", g_ev)
                wfree[s2_] = ("G", g_ev)
                use_bias = dst is cos_bf
                trig_cmds.append([("G", g_ev), dst, off, use_bias])
                winit_last_g = g_ev
            else:
                wo_cast_jobs.append((src0, dst, off))

        ev_xsub_free = [0, 0]
        ev_xsub_sp_s = 0
        ev_xT_free = [0, 0]
        ev_xn32_free = ("V", 0)
        ev_xn32_free_s = 0
        ev_xnbf_free_t = 0
        bfree_s = {}
        bfree_v = {}
        ev_delta_rd_v = [("V", 0), ("V", 0)]
        ev_delta_rd_s = [0, 0]
        ev_v_free = [0, 0]
        ev_abar_free = [0, 0, 0]
        ev_g_free_t = [0, 0, 0, 0]
        ev_g_free_v = [("V", 0)] * 4
        ev_sfm_free_t = [0, 0]
        ev_osub_free = [0, 0]
        ev_xres_free = 0
        ev_ybf_rd_t = 0
        ev_et_rd_s = [0, 0]
        ev_et_wr_g = [0, 0]
        d_alt = 0
        prev_vts = 0
        prev_s1 = 0
        prev_s2 = 0
        ev_init_v = 0
        ev_init_s = 0
        ev_p0_slots = {}

        ev_xres_free2 = [("V", 0), ("V", 0)]
        sp_mm = {}
        sp_bank = {}
        sp_done = {}
        vv_done = {}
        yev_mm = {}
        yev_last = 0

        def emit_sp_mm(p, dt, banks=None, keys=None):
            if banks is None:
                banks = pd
                keys = [("pd", 0), ("pd", 1)]
            for c in range(2):
                w = 384 if c == 0 else TPL[p] - 384
                wait("tensor", "S", a_xnbf_s[p])
                wait("tensor", "S", bfree_s.get(keys[c], 0))
                wait("tensor", "V", bfree_v.get(keys[c], 0))
                gdw = grp_done[id(wd_bf)]
                if isinstance(gdw, list):
                    for sm_, evv_ in gdw:
                        wait("tensor", sm_, evv_)
                else:
                    wait("tensor", "G", gdw)
                tmm = None
                for k in range(NDT):
                    tmm = run("tensor", lambda e, k=k, dt=dt, c=c, w=w, bk=banks[c]: e.matmul(
                        bk[:, 0:w],
                        wd_bf[:, k * 1024 + dt * P:k * 1024 + (dt + 1) * P],
                        xnbf[:, k * TP + c * 384:k * TP + c * 384 + w],
                        start=(k == 0), stop=(k == NDT - 1)),
                        "T" if k == NDT - 1 else None)
                sp_mm[(dt, c)] = tmm
                sp_bank[(dt, c)] = (banks[c], keys[c])

        def emit_spE(p, dt, c):
            nonlocal ev_xsub_sp_s
            wait("scalar", "T", sp_mm[(dt, c)])
            wait("scalar", "LI", 16)
            wait("scalar", "S", ev_xsub_sp_s)
            wait("scalar", "V", ev_xsub_free[0])
            if p == 0:
                wait("scalar", "T", ev_p0_slots.get(0, 0))
            w = 384 if c == 0 else TPL[p] - 384
            bk, key = sp_bank[(dt, c)]
            se = run("scalar", lambda e, c=c, dt=dt, w=w, bk=bk: e.activation(
                out=xsub[:, 0:w], in_=bk[:, 0:w], func=AF.Exp,
                bias=bdc_s[:, dt:dt + 1]), "S")
            bfree_s[key] = se
            return se

        def emit_spL(p, dt, c, se):
            nonlocal ev_xsub_sp_s
            par = dt % 2
            wait("scalar", "S", se)
            wait("scalar", ev_delta_rd_v[par][0], ev_delta_rd_v[par][1])
            wait("scalar", "S", ev_delta_rd_s[par])
            if p == 0:
                wait("scalar", "T", ev_p0_slots.get(2, 0))
            for rj in (5, 6, 7):
                wait("scalar", "V", ring_stt[rj])
            w = 384 if c == 0 else TPL[p] - 384
            sl = run("scalar", lambda e, par=par, c=c, w=w: e.activation(
                out=delta[:, par * TP + c * 384:par * TP + c * 384 + w],
                in_=xsub[:, 0:w], func=AF.Ln, bias=1.0), "S")
            ev_xsub_sp_s = sl
            return sl

        def emit_vmult(p, dt):
            par = dt % 2
            use_g = False
            eng = "gpsimd" if use_g else "vector"
            sem = "G" if use_g else "V"
            wait(eng, "S", sp_done[dt])
            wait(eng, "V", ev_v_free[par])
            wait(eng, "V", a_x32_v[p])
            if p == 0:
                wait(eng, "T", ev_p0_slots.get(5, 0))
            if p == 0 and dt < 2:
                wait(eng, "S", ev_wds_s[0])
            for rj in (8, 9, 10):
                wait(eng, "V", ring_stt[rj])
            tp = TPL[p]
            vv = run(eng, lambda e, par=par, dt=dt, tp=tp: e.tensor_tensor(
                out=vbuf[:, par * TP:par * TP + tp],
                in0=delta[:, par * TP:par * TP + tp],
                in1=xn32[:, dt * TP:dt * TP + tp], op=ALU.mult), sem)
            ev_delta_rd_v[par] = (sem, vv)
            vv_done[dt] = (sem, vv)

        a_xnbf_s = {}
        a_x32_v = {}

        def emit_transposes(p, ts, par, vts, src_ap, banks=None, keys=None):
            nonlocal ev_xn32_free, ev_xn32_free_s
            if banks is None:
                banks = [pt[0], pt[1]]
                keys = [("pt", 0), ("pt", 1)]
            last_s = last_v = 0
            for half in range(2):
                wait("tensor", "LI", 16)
                wait("tensor", "V", vts)
                wait("tensor", "V", bfree_v.get(keys[half], 0))
                wait("tensor", "S", bfree_s.get(keys[half], 0))
                tev = None
                for j in range(4):
                    dt = half * 4 + j
                    tev = run("tensor", lambda e, bk=banks[half], j=j, dt=dt, a=src_ap: e.transpose(
                        bk[:, j * P:(j + 1) * P],
                        a[:, dt * P:(dt + 1) * P],
                        idn_s), "T")
                ev_xT_free[par] = tev
                wait("vector", "T", tev)
                wait("vector", ev_xn32_free[0], ev_xn32_free[1])
                wait("vector", "S", ev_xn32_free_s)
                x32ap = xn32[:].rearrange("q (d t) -> q d t", d=NDT)[
                    :, half * 4:(half + 1) * 4, ts * P:(ts + 1) * P]
                vev = run("vector", lambda e, a=x32ap, bk=banks[half]: e.tensor_copy(
                    out=a, in_=bk[:, 0:512]), "V")
                wait("scalar", "V", vev)
                wait("scalar", "T", ev_xnbf_free_t)
                xbfap = xnbf[:].rearrange("q (d t) -> q d t", d=NDT)[
                    :, half * 4:(half + 1) * 4, ts * P:(ts + 1) * P]
                sev = run("scalar", lambda e, a=xbfap, b=x32ap: e.activation(
                    out=a, in_=b, func=AF.Copy), "S")
                bfree_v[keys[half]] = vev
                last_s = sev
                last_v = vev
            return last_s, last_v

        a_last = [0, 0]

        def emit_a_subtile(p, ts, banks=None, keys=None):
            nonlocal prev_vts, prev_s1, prev_s2
            par = ts % 2
            lxs = "LX0" if par == 0 else "LX1"
            wait("sync", "V", ev_xsub_free[par])
            wait("sync", "T", ev_xT_free[par])
            if par == 0:
                wait("sync", "S", ev_xsub_sp_s)
                wait("sync", "V", ring_stt[0])
            else:
                wait("sync", "V", ring_stt[1])
                wait("sync", "V", ring_stt[2])
            lx = run("sync", lambda e, r=(SUBBASE[p] + ts) * P, par=par: e.dma_start(
                out=xsub[:, par * 1024:(par + 1) * 1024], in_=xin[r:r + P, :]),
                lxs, 16)
            wait("vector", lxs, lx)
            wait("vector", "V", prev_vts)
            run("vector", lambda e, par=par: e.bn_stats(
                out=bn6[:, 0:6], in_=xsub[:, par * 1024:par * 1024 + 512]), "V")
            vst = run("vector", lambda e, par=par: e.bn_stats(
                out=bn6[:, 6:12], in_=xsub[:, par * 1024 + 512:par * 1024 + 1024]), "V")
            wait("vector", "V", vst)
            wait("vector", "S", prev_s1)
            vag = run("vector", lambda e: e.bn_aggr(out=bn2[:], in_=bn6[:, 0:12]), "V")
            wait("scalar", "V", vag)
            wait("scalar", "S", prev_s2)
            wait("scalar", "V", prev_vts)
            s1 = run("scalar", lambda e: e.activation(
                out=lnv[:, 0:1], in_=bn2[:, 1:2], func=AF.Ln, bias=cvals[:, 1:2]), "S")
            wait("scalar", "S", s1)
            s2 = run("scalar", lambda e: e.activation(
                out=rstd[:], in_=lnv[:, 0:1], func=AF.Exp, scale=-0.5), "S")
            wait("vector", "S", s2)
            v1 = run("vector", lambda e: e.scalar_tensor_tensor(
                out=negmr[:], in0=bn2[:, 0:1], scalar=cvals[:, 2:3], in1=rstd[:],
                op0=ALU.mult, op1=ALU.mult), "V")
            wait("vector", "V", v1)
            vts = run("vector", lambda e, par=par: e.tensor_scalar(
                out=xsub[:, par * 1024:(par + 1) * 1024],
                in0=xsub[:, par * 1024:(par + 1) * 1024],
                scalar1=rstd[:], scalar2=negmr[:],
                op0=ALU.mult, op1=ALU.add), "V")
            ev_xsub_free[par] = vts
            prev_vts, prev_s1, prev_s2 = vts, s1, s2
            a_last[0], a_last[1] = emit_transposes(
                p, ts, par, vts, xsub[:, par * 1024:(par + 1) * 1024], banks, keys)

        def emit_stage_a(p):
            nonlocal prev_vts, prev_s1, prev_s2, ev_init_v, ev_init_s
            last_s = last_v = 0
            if p == 0:
                xslots = p0_xslots
                xsems = p0_xsems
                lxev = p0_lxev
                nsub0 = NSUBL[0]
                vag = 0
                for ts in range(nsub0):
                    wait("vector", xsems[ts], lxev[ts])
                    run("vector", lambda e, ts=ts, a=xslots[ts]: e.bn_stats(
                        out=bn6[:, ts * 12:ts * 12 + 6], in_=a[:, 0:512]), "V")
                    vst = run("vector", lambda e, ts=ts, a=xslots[ts]: e.bn_stats(
                        out=bn6[:, ts * 12 + 6:ts * 12 + 12], in_=a[:, 512:1024]), "V")
                    wait("vector", "V", vst)
                    vag = run("vector", lambda e, ts=ts: e.bn_aggr(
                        out=bnA[:, ts * 2:ts * 2 + 2],
                        in_=bn6[:, ts * 12:ts * 12 + 12]), "V")
                var_ap = bnA[:].rearrange("q (s t) -> q s t", t=2)[:, 0:nsub0, 1:2]
                mean_ap = bnA[:].rearrange("q (s t) -> q s t", t=2)[:, 0:nsub0, 0:1]
                wait("scalar", "V", vag)
                s1 = run("scalar", lambda e, a=var_ap, w=nsub0: e.activation(
                    out=lnv[:, 0:w], in_=a, func=AF.Ln, bias=cvals[:, 1:2]), "S")
                wait("scalar", "S", s1)
                s2 = run("scalar", lambda e, w=nsub0: e.activation(
                    out=rstd6[:, 0:w], in_=lnv[:, 0:w], func=AF.Exp,
                    scale=-0.5), "S")
                wait("vector", "S", s2)
                v1 = run("vector", lambda e, a=mean_ap, w=nsub0: e.scalar_tensor_tensor(
                    out=negmr6[:, 0:w], in0=a, scalar=cvals[:, 2:3],
                    in1=rstd6[:, 0:w], op0=ALU.mult, op1=ALU.mult), "V")
                prev_vts, prev_s1, prev_s2 = v1, s1, s2
                wait("scalar", "LAC", lacl)
                sa = run("scalar", lambda e: e.activation(
                    out=osub[:, 0:128], in_=nega[:], func=AF.Exp), "S")
                wait("scalar", "S", sa)
                sa = run("scalar", lambda e: e.activation(
                    out=osub[:, 128:256], in_=osub[:, 0:128], func=AF.Ln, bias=1.0), "S")
                wait("vector", "S", sa)
                v_na = run("vector", lambda e: e.tensor_scalar(
                    out=nega[:], in0=osub[:, 128:256], scalar1=1e-4, scalar2=-1.0,
                    op0=ALU.add, op1=ALU.mult), "V")
                ev_init_v = v_na
                ev_init_s = sa
                for ts in range(NSUBL[0]):
                    par = ts % 2
                    wait("vector", "V", v1)
                    wait("vector", "S", s2)
                    vts = run("vector", lambda e, ts=ts, a=xslots[ts]: e.tensor_scalar(
                        out=a, in0=a, scalar1=rstd6[:, ts:ts + 1],
                        scalar2=negmr6[:, ts:ts + 1],
                        op0=ALU.mult, op1=ALU.add), "V")
                    last_s, last_v = emit_transposes(p, ts, par, vts, xslots[ts])
                    ev_p0_slots[ts] = ev_xT_free[par]
                    if ts < 2:
                        ev_xsub_free[ts] = vts
                a_xnbf_s[p] = last_s
                a_x32_v[p] = last_v
            else:
                for ts in range(NSUB):
                    emit_a_subtile(p, ts)
                a_xnbf_s[p] = a_last[0]
                a_x32_v[p] = a_last[1]
                return

        ring_ap = ([xsub[:, 512 + j * 512:1024 + j * 512] for j in range(3)]
                   + [etmp[:, 0:512], etmp[:, 512:1024]]
                   + [delta[:, j * 512:(j + 1) * 512] for j in range(3)]
                   + [vbuf[:, j * 512:(j + 1) * 512] for j in range(3)]
                   + [abar[:, j * 512:(j + 1) * 512] for j in range(3)])
        NRING = len(ring_ap)
        ring_stt = [0] * NRING
        ring_g = [0] * NRING
        gadd_ev = [winit_last_g, winit_last_g]

        a_done_set = [set() for _ in range(NPH + 1)]
        presp = [False] * (NPH + 1)
        presc = [False] * (NPH + 1)
        hoist_sx = {}
        hoist_vs = {}

        def emit_h_exp(pn, n):
            tpn = TPL[pn]
            sl = n % 3
            for rj in {0: (11, 12), 1: (12, 13), 2: ()}[sl]:
                wait("scalar", "V", ring_stt[rj])
            wait("scalar", "V", ev_abar_free[sl])
            wait("scalar", "S", sp_done[0])
            sx = run("scalar", lambda e, sl=sl, n=n, tp=tpn: e.activation(
                out=abar[:, sl * TP:sl * TP + tp],
                in_=delta[:, 0:tp], func=AF.Exp,
                scale=nega[:, n:n + 1]), "S")
            hoist_sx[n] = sx
            ev_delta_rd_s[0] = sx

        def emit_h_scan(pn, n):
            tpn = TPL[pn]
            sl = n % 3
            gsl = n % 4
            wait("vector", "S", hoist_sx[n])
            wait("vector", vv_done[0][0], vv_done[0][1])
            wait("vector", "T", ev_g_free_t[gsl])
            wait("vector", ev_g_free_v[gsl][0], ev_g_free_v[gsl][1])
            vs = run("vector", lambda e, g=gsl, sl=sl, n=n, tp=tpn: e.tensor_tensor_scan(
                out=grot[:, g * TP:g * TP + tp],
                data0=abar[:, sl * TP:sl * TP + tp],
                data1=vbuf[:, 0:tp],
                initial=gcar[:, n:n + 1],
                op0=ALU.mult, op1=ALU.add), "V")
            hoist_vs[n] = vs
            ev_abar_free[sl] = vs
            ev_v_free[0] = vs
            if n == 3:
                wait("vector", "V", vs)
                vc = run("vector", lambda e, tp=tpn: e.tensor_copy(
                    out=gcar[:, 0:4],
                    in_=grot[:].rearrange("q (s t) -> q s t", s=4)[:, :, tp - 1:tp]), "V")
                for s4 in range(4):
                    ev_g_free_v[s4] = ("V", vc)

        for p in range(NPH):
            tp = TPL[p]
            if p == 0:
                emit_stage_a(0)
                for (q_, la_, dst_, off_) in wd_s_jobs:
                    wait("scalar", wds_sems[q_], la_)
                    s_ev = run("scalar", lambda e, d=dst_, o=off_, q=q_: e.activation(
                        out=d[:, o:o + 512], in_=wds_slots[q], func=AF.Copy), "S")
                    ev_wds_s[0] = s_ev
                    if q_ == 3:
                        ev_xres_free2[0] = ("S", s_ev)
                    elif q_ == 4:
                        ev_xres_free2[1] = ("S", s_ev)
                grp_done[id(wd_bf)] = [("G", grp_done[id(wd_bf)]),
                                       ("S", ev_wds_s[0])]
                ev_init_s = max(ev_init_s, ev_wds_s[0])
                for dtq in range(2, 6):
                    bdg_ev[dtq] = ("S", emit_diag_s(dtq, 0, NST))
            else:
                for ts in range(NSUBL[p]):
                    if ts not in a_done_set[p]:
                        emit_a_subtile(p, ts,
                                       [pe[1], py[1]], [("pe", 1), ("py", 1)])
                        a_done_set[p].add(ts)
                a_xnbf_s[p] = a_last[0]
                a_x32_v[p] = a_last[1]
            if not presp[p]:
                emit_sp_mm(p, 0)
                se0 = emit_spE(p, 0, 0)
                sl0 = emit_spL(p, 0, 0, se0)
                se1 = emit_spE(p, 0, 1)
                sp_done[0] = emit_spL(p, 0, 1, se1)
                emit_vmult(p, 0)

            chunks = [(W, 512)] if p == 0 else [(0, 384), (384, 384)]
            nch = NDT * len(chunks)
            wo_pend = []
            cjobs = [(dt, c0, wdth) for (c0, wdth) in chunks
                     for dt in range(NDT)]
            mag_exp = [None] * nch
            ffn_ti = [0]

            def ring_avail_waits(j, eng):
                if j < 3:
                    par = 0 if j == 0 else 1
                    wait(eng, "T", ev_xT_free[par])
                    wait(eng, "V", ev_xsub_free[par])
                elif j < 5:
                    wait(eng, "G", winit_last_g)
                    wait(eng, "V", winit_vec[0])
                elif j < 8:
                    if j == 5:
                        wait(eng, "S", ev_delta_rd_s[0])
                        wait(eng, ev_delta_rd_v[0][0], ev_delta_rd_v[0][1])
                    else:
                        wait(eng, "S", ev_delta_rd_s[0])
                        wait(eng, "S", ev_delta_rd_s[1])
                        wait(eng, ev_delta_rd_v[0][0], ev_delta_rd_v[0][1])
                        wait(eng, ev_delta_rd_v[1][0], ev_delta_rd_v[1][1])
                elif j < 11:
                    if j == 8:
                        wait(eng, "V", ev_v_free[0])
                    else:
                        wait(eng, "V", ev_v_free[0])
                        wait(eng, "V", ev_v_free[1])
                else:
                    wait(eng, "V", ev_abar_free[0])
                    wait(eng, "V", ev_abar_free[1])
                    wait(eng, "V", ev_abar_free[2])

            ffn_mm = {}

            def emit_ffn_mm(i):
                dt, c0, wdth = cjobs[i]
                wait("tensor", "S", trig_s_last[0] if dt < 4 else trig_h1[0])
                wait("tensor", "S", bfree_s.get(("pt", 0), 0))
                wait("tensor", "V", bfree_v.get(("pt", 0), 0))
                tr = None
                for k in range(NDT):
                    tr = run("tensor", lambda e, k=k, dt=dt, c0=c0, w=wdth: e.matmul(
                        pt[0][:, 0:w],
                        cos_bf[:, k * 1024 + dt * P:k * 1024 + (dt + 1) * P],
                        xnbf[:, k * TP + c0:k * TP + c0 + w],
                        start=(k == 0), stop=(k == NDT - 1)),
                        "T" if k == NDT - 1 else None)
                wait("tensor", "S", bfree_s.get(("pt", 1), 0))
                wait("tensor", "V", bfree_v.get(("pt", 1), 0))
                ti = None
                for k in range(NDT):
                    ti = run("tensor", lambda e, k=k, dt=dt, c0=c0, w=wdth: e.matmul(
                        pt[1][:, 0:w],
                        sin_bf[:, k * 1024 + dt * P:k * 1024 + (dt + 1) * P],
                        xnbf[:, k * TP + c0:k * TP + c0 + w],
                        start=(k == 0), stop=(k == NDT - 1)),
                        "T" if k == NDT - 1 else None)
                ffn_ti[0] = ti
                ffn_mm[i] = (tr, ti)

            def emit_ffn_mm_part(i, g):
                dt_, c0, wdth = cjobs[i]
                im = g // 2
                half = g % 2
                src = cos_bf if im == 0 else sin_bf
                bank = pt[im]
                if half == 0:
                    wait("tensor", "S", trig_s_last[0] if dt_ < 4 else trig_h1[0])
                    wait("tensor", "S", bfree_s.get(("pt", im), 0))
                    wait("tensor", "V", bfree_v.get(("pt", im), 0))
                tl = None
                for k in range(half * 4, half * 4 + 4):
                    tl = run("tensor", lambda e, k=k, dt=dt_, c0=c0, w=wdth, b=bank, s=src: e.matmul(
                        b[:, 0:w],
                        s[:, k * 1024 + dt * P:k * 1024 + (dt + 1) * P],
                        xnbf[:, k * TP + c0:k * TP + c0 + w],
                        start=(k == 0), stop=(k == NDT - 1)),
                        "T" if k == NDT - 1 else None)
                if g == 1:
                    ffn_mm[i] = (tl, None)
                elif g == 3:
                    ffn_mm[i] = (ffn_mm[i][0], tl)
                    ffn_ti[0] = tl

            def rslot(i):
                m = {5: 5, 6: 8, 7: 6, 8: 7, 9: 9}
                if i < 5:
                    return i
                if i in m:
                    return m[i]
                if i < NRING:
                    return i
                return i - NRING + 3

            def emit_ffn_chain(i):
                dt, c0, wdth = cjobs[i]
                j = rslot(i)
                li = i % 2
                tr, ti = ffn_mm[i]
                wait("scalar", "T", tr)
                wait("scalar", "G", gadd_ev[li])
                wait("scalar", "V", winit_vec[0])
                sq1 = run("scalar", lambda e, w=wdth, li=li: e.activation(
                    out=r2[:, li * 512:li * 512 + w], in_=pt[0][:, 0:w],
                    func=AF.Square), "S")
                bfree_s[("pt", 0)] = sq1
                wait("scalar", "T", ti)
                sq2 = run("scalar", lambda e, w=wdth, li=li: e.activation(
                    out=i2[:, li * 512:li * 512 + w], in_=pt[1][:, 0:w],
                    func=AF.Square), "S")
                bfree_s[("pt", 1)] = sq2
                wait("gpsimd", "S", sq2)
                wait("gpsimd", "V", ring_stt[j])
                wait("gpsimd", "G", ring_g[j])
                ring_avail_waits(j, "gpsimd")
                ga = run("gpsimd", lambda e, w=wdth, li=li, a=ring_ap[j]: e.tensor_tensor(
                    out=a[:, 0:w], in0=r2[:, li * 512:li * 512 + w],
                    in1=i2[:, li * 512:li * 512 + w], op=ALU.add), "G")
                gadd_ev[li] = ga
                ring_g[j] = ga
                wait("scalar", "G", ga)
                sln = run("scalar", lambda e, w=wdth, a=ring_ap[j]: e.activation(
                    out=a[:, 0:w], in_=a[:, 0:w], func=AF.Ln), "S")
                wait("scalar", "S", sln)
                mag_exp[i] = run("scalar", lambda e, w=wdth, a=ring_ap[j]: e.activation(
                    out=a[:, 0:w], in_=a[:, 0:w], func=AF.Exp, scale=0.5), "S")

            ffn_at = ({6: [0], 7: [1]} if p == 0
                      else {0: [0], 1: [1], 2: [2], 3: [3], 4: [4],
                            5: [], 6: [5], 7: [6]})
            NFC = 2 if p == 0 else 7
            yev_pend = []
            for dt in range(NDT):
                par = dt % 2
                if dt + 1 < NDT:
                    emit_sp_mm(p, dt + 1)
                if dt >= 5:
                    for jb in ffn_at.get(dt, []):
                        emit_ffn_mm(jb)
                aux = []
                for (ydt, c_) in yev_pend:
                    aux.append(("yev", ydt, c_))
                yev_pend = []
                if dt + 1 < NDT:
                    aux.append(("spE0", dt + 1, 0))
                    aux.append(("spL0", dt + 1, 0))
                    aux.append(("spE1", dt + 1, 1))
                    aux.append(("spL1", dt + 1, 1))
                if p == 0 and dt < 2:
                    for q_ in range(4):
                        aux.append(("diag", dt + 6, q_))
                if p == 0 and 2 <= dt < 6:
                    aux.append(("sinc", (dt - 2) * 4, 4))
                if p == 0 and dt in (0, 1, 6, 7):
                    q1 = {0: 0, 1: 1, 6: 2, 7: 3}[dt]
                    aux.append(("sin1", q1 * 4, 4))
                if dt < 5:
                    for jb in ffn_at.get(dt, []):
                        for g_ in range(4):
                            aux.append(("fmg", jb, g_))
                assert len(aux) <= NST - 1
                aux_i = 0
                sp_se = {}
                vs = None
                sx = None
                td_c = [0, 0]
                for n in range(NST):
                    sl_ab = n % 3
                    gsl = n % 4
                    hoisted = presc[p] and dt == 0 and n < 4
                    if not hoisted:
                        wait("scalar", "V", ev_abar_free[sl_ab])
                        wait("scalar", "V", ev_init_v)
                        wait("scalar", "S", sp_done[dt])
                        if p == 0 and dt == 0 and n <= 1:
                            wait("scalar", "T", ev_p0_slots.get(3, 0))
                        if n <= 1:
                            for rj in (11, 12, 13):
                                wait("scalar", "V", ring_stt[rj])
                        sx = run("scalar", lambda e, sl=sl_ab, par=par, dt=dt, n=n, tp=tp: e.activation(
                            out=abar[:, sl * TP:sl * TP + tp],
                            in_=delta[:, par * TP:par * TP + tp], func=AF.Exp,
                            scale=nega[:, dt * NST + n:dt * NST + n + 1]), "S")
                    if n >= 1 and aux_i < len(aux):
                        kind, adt, ac = aux[aux_i]
                        aux_i += 1
                        if kind == "yev":
                            wait("scalar", "T", yev_mm[(adt, ac)])
                            wait("scalar", "T", ev_ybf_rd_t)
                            wdth_y = tp - W if p == 0 else 384
                            off_y = W if p == 0 else ac * 384
                            ybank = (pe if adt % 2 else py)[ac]
                            sy = run("scalar", lambda e, dt2=adt, o=off_y, w=wdth_y, bb=ybank: e.activation(
                                out=ybf[:, dt2 * TP + o:dt2 * TP + o + w],
                                in_=bb[:, 0:w], func=AF.Copy), "S")
                            bfree_s[("pe" if adt % 2 else "py", ac)] = sy
                            yev_last = sy
                        elif kind == "spE0":
                            sp_se[0] = emit_spE(p, adt, 0)
                        elif kind == "spL0":
                            emit_spL(p, adt, 0, sp_se[0])
                        elif kind == "spE1":
                            sp_se[1] = emit_spE(p, adt, 1)
                        elif kind == "spL1":
                            sp_done[adt] = emit_spL(p, adt, 1, sp_se[1])
                        elif kind in ("sinc", "sin1"):
                            idxs = (h0_idx if kind == "sinc" else h1_idx)[adt:adt + ac]
                            ttrg = trig_s_last if kind == "sinc" else trig_h1
                            for t_ in idxs:
                                (gsm, g_ev), _, _, _ = trig_cmds[t_]
                                wait("scalar", gsm, g_ev)
                            wait("scalar", "V", v_cv)
                            for t_ in idxs:
                                _, dst_, off_, ub_ = trig_cmds[t_]
                                ttrg[0] = run("scalar", lambda e, d=dst_, o=off_, ub=ub_: e.activation(
                                    out=d[:, o:o + 512], in_=d[:, o:o + 512], func=AF.Sin,
                                    bias=(cvals[:, 0:1] if ub else 0.0), scale=1.0), "S")
                        elif kind == "diag":
                            ev_d = emit_diag_s(adt, ac * 4, ac * 4 + 4)
                            if ac == 3:
                                bdg_ev[adt] = ("S", ev_d)
                        elif kind == "fmg":
                            emit_ffn_mm_part(adt, ac)
                    if hoisted:
                        vs = hoist_vs[n]
                    else:
                        wait("vector", "S", sx)
                        wait("vector", vv_done[dt][0], vv_done[dt][1])
                        wait("vector", "T", ev_g_free_t[gsl])
                        wait("vector", ev_g_free_v[gsl][0], ev_g_free_v[gsl][1])
                        vs = run("vector", lambda e, g=gsl, sl=sl_ab, par=par, dt=dt, n=n, tp=tp: e.tensor_tensor_scan(
                            out=grot[:, g * TP:g * TP + tp],
                            data0=abar[:, sl * TP:sl * TP + tp],
                            data1=vbuf[:, par * TP:par * TP + tp],
                            initial=gcar[:, dt * NST + n:dt * NST + n + 1],
                            op0=ALU.mult, op1=ALU.add), "V")
                        ev_abar_free[sl_ab] = vs
                        if n % 4 == 3:
                            g0 = n - 3
                            wait("vector", "V", vs)
                            gs = grot[:].rearrange("q (s t) -> q s t", s=4)[:, :, tp - 1:tp]
                            vc = run("vector", lambda e, a=gs, dt=dt, g0=g0: e.tensor_copy(
                                out=gcar[:, dt * NST + g0:dt * NST + g0 + 4], in_=a), "V")
                            for s4 in range(4):
                                ev_g_free_v[s4] = ("V", vc)
                    if n == 7 and dt + 1 < NDT:
                        emit_vmult(p, dt + 1)
                    ypair = pe if dt % 2 else py
                    ypname = "pe" if dt % 2 else "py"
                    wait("tensor", bdg_ev[dt][0], bdg_ev[dt][1])
                    if p == 0:
                        wait("tensor", "V", vs)
                        if n == 0:
                            wait("tensor", "S", bfree_s.get((ypname, 0), 0))
                            wait("tensor", "V", bfree_v.get((ypname, 0), 0))
                        td = run("tensor", lambda e, g=gsl, dt=dt, n=n, bb=ypair[0], tp=tp: e.matmul(
                            bb[:, 0:tp - W],
                            bdg_bf[:, (dt * NST + n) * P:(dt * NST + n + 1) * P],
                            grot[:, g * TP + W:g * TP + tp],
                            start=(n == 0), stop=(n == NST - 1)), "T")
                        td_c[0] = td
                    else:
                        td = None
                        for c in range(2):
                            wait("tensor", "V", vs)
                            if n == 0:
                                wait("tensor", "S", bfree_s.get((ypname, c), 0))
                                wait("tensor", "V", bfree_v.get((ypname, c), 0))
                            td = run("tensor", lambda e, c=c, g=gsl, dt=dt, n=n, bb=ypair[c]: e.matmul(
                                bb[:, 0:384],
                                bdg_bf[:, (dt * NST + n) * P:(dt * NST + n + 1) * P],
                                grot[:, g * TP + c * 384:g * TP + (c + 1) * 384],
                                start=(n == 0), stop=(n == NST - 1)), "T")
                        td_c[1] = td
                        td_c[0] = td
                    ev_g_free_t[gsl] = td
                ev_v_free[par] = vs
                ev_delta_rd_s[par] = sx
                for c in ([0] if p == 0 else [0, 1]):
                    yev_mm[(dt, c)] = td_c[c]
                    yev_pend.append((dt, c))
                for jb in ffn_at.get(dt, []):
                    emit_ffn_chain(jb)
                if p == 0:
                    for _ in range(3):
                        if wo_cast_jobs:
                            s0_, dst_, off_ = wo_cast_jobs.pop(0)
                            sl = wslot_next()
                            wait("sync", *wfree[sl])
                            la = run("sync", lambda e, s=s0_, sl=sl: e.dma_start(
                                out=wslot[sl], in_=s), wsem[sl], 16)
                            wo_pend.append((sl, la, dst_, off_))
                    while len(wo_pend) > 3 or (not wo_cast_jobs and wo_pend):
                        sl, la, dst_, off_ = wo_pend.pop(0)
                        wait("vector", wsem[sl], la)
                        v_ev = run("vector", lambda e, d=dst_, o=off_, sl=sl: e.tensor_copy(
                            out=d[:, o:o + 512], in_=wslot[sl]), "V")
                        wfree[sl] = ("V", v_ev)
                        winit_vec[0] = v_ev
                        grp_done[id(dst_)] = ("V", v_ev)

            for (ydt, c_) in yev_pend:
                wait("scalar", "T", yev_mm[(ydt, c_)])
                wait("scalar", "T", ev_ybf_rd_t)
                wdth_y = tp - W if p == 0 else 384
                off_y = W if p == 0 else c_ * 384
                ybank = (pe if ydt % 2 else py)[c_]
                sy = run("scalar", lambda e, dt2=ydt, o=off_y, w=wdth_y, bb=ybank: e.activation(
                    out=ybf[:, dt2 * TP + o:dt2 * TP + o + w],
                    in_=bb[:, 0:w], func=AF.Copy), "S")
                bfree_s[("pe" if ydt % 2 else "py", c_)] = sy
                yev_last = sy
            yev_pend = []

            ev_xn32_free = vv_done[NDT - 1]
            ev_xn32_free_s = a_xnbf_s[p]
            if p == 0:
                a_sched = {1: [0], 3: [5], 6: [1], 7: [2, 3]}
            else:
                a_sched = {1: [0], 3: [1], 4: [2],
                           9: [3], 10: [4], 11: [5]}
            for it in range(nch):
                dt, c0, wdth = cjobs[it]
                if NFC + it < nch:
                    emit_ffn_mm(NFC + it)
                b = it % 2
                tok0 = PSTART[p] + c0
                xrs = it % 2
                lrs = "LR" if xrs == 0 else "LR1"
                wait("sync", ev_xres_free2[xrs][0], ev_xres_free2[xrs][1])
                lr = run("sync", lambda e, t0=tok0, dt=dt, w=wdth, xrs=xrs: e.dma_start(
                    out=xres[:, xrs * 512:xrs * 512 + w],
                    in_=xin[t0:t0 + w, dt * P:(dt + 1) * P].rearrange(
                        "(j q) r -> q j r", q=P)), lrs, 16)
                if NFC + it < nch:
                    emit_ffn_chain(NFC + it)
                gd = grp_done[id(wo_bf)]
                if isinstance(gd, tuple):
                    wait("tensor", gd[0], gd[1])
                else:
                    wait("tensor", "G", gd)
                wait("tensor", "S", yev_last)
                wait("tensor", "S", bfree_s.get(("pd", b), 0))
                wait("tensor", "V", bfree_v.get(("pd", b), 0))
                tw = None
                for k in range(NDT):
                    tw = run("tensor", lambda e, k=k, dt=dt, c0=c0, w=wdth, b=b: e.matmul(
                        pd[b][:, 0:w],
                        wo_bf[:, k * 1024 + dt * P:k * 1024 + (dt + 1) * P],
                        ybf[:, k * TP + c0:k * TP + c0 + w],
                        start=(k == 0), stop=(k == NDT - 1)),
                        "T" if k == NDT - 1 else None)
                if p + 1 < NPH:
                    for ts_ in a_sched.get(it, []):
                        if ts_ not in a_done_set[p + 1]:
                            ev_xnbf_free_t = ffn_ti[0]
                            emit_a_subtile(p + 1, ts_,
                                           [pe[1], py[1]], [("pe", 1), ("py", 1)])
                            a_done_set[p + 1].add(ts_)
                slot = it % 2
                jr = rslot(it)
                wait("vector", "T", tw)
                wait("vector", "S", mag_exp[it])
                wait("vector", "LI", 16)
                wait("vector", "T", ev_sfm_free_t[slot])
                if p == 0:
                    wait("vector", "T", ev_p0_slots.get(4, 0))
                vstt = run("vector", lambda e, sl=slot, dt=dt, w=wdth, b=b, a=ring_ap[jr]: e.scalar_tensor_tensor(
                    out=sfm[:, sl * 512:sl * 512 + w], in0=pd[b][:, 0:w],
                    scalar=boc_s[:, dt:dt + 1], in1=a[:, 0:w],
                    op0=ALU.add, op1=ALU.add), "V")
                bfree_v[("pd", b)] = vstt
                ring_stt[jr] = vstt
                tb = pe[0] if it % 2 == 0 else py[0]
                tbn = ("pe", 0) if it % 2 == 0 else ("py", 0)
                nj = wdth // P
                wait("tensor", "V", vstt)
                wait("tensor", "S", bfree_s.get(tbn, 0))
                wait("tensor", "V", bfree_v.get(tbn, 0))
                tt = None
                for jj in range(nj):
                    tt = run("tensor", lambda e, bk=tb, jj=jj, sl=slot: e.transpose(
                        bk[:, jj * P:(jj + 1) * P],
                        sfm[:, sl * 512 + jj * P:sl * 512 + (jj + 1) * P],
                        idn_s), "T")
                ev_sfm_free_t[slot] = tt
                osl = it % 2
                osem = "O0" if osl == 0 else "O1"
                wait("vector", "T", tt)
                wait("vector", lrs, lr)
                wait("vector", osem, ev_osub_free[osl])
                if p == 0 and it < 2:
                    wait("vector", "S", ev_init_s)
                vo = run("vector", lambda e, bk=tb, osl=osl, w=wdth, xrs=xrs: e.tensor_add(
                    out=osub[:, osl * 512:osl * 512 + w],
                    in0=bk[:, 0:w],
                    in1=xres[:, xrs * 512:xrs * 512 + w]), "V")
                ev_xres_free2[xrs] = ("V", vo)
                bfree_v[tbn] = vo
                wait("gpsimd", "V", vo)
                oe = run("gpsimd", lambda e, t0=tok0, dt=dt, w=wdth, osl=osl: e.dma_start(
                    out=out[t0 - W:t0 - W + w, dt * P:(dt + 1) * P].rearrange(
                        "(j q) r -> q j r", q=P),
                    in_=osub[:, osl * 512:osl * 512 + w]), osem, 16)
                ev_osub_free[osl] = oe
                if p == 1 and it == 12:
                    a_xnbf_s[p + 1] = a_last[0]
                    a_x32_v[p + 1] = a_last[1]
                    emit_sp_mm(p + 1, 0, pt, [("pt", 0), ("pt", 1)])
                    se0_ = emit_spE(p + 1, 0, 0)
                    emit_spL(p + 1, 0, 0, se0_)
                    se1_ = emit_spE(p + 1, 0, 1)
                    sp_done[0] = emit_spL(p + 1, 0, 1, se1_)
                    emit_vmult(p + 1, 0)
                    presp[p + 1] = True
                if p == 1:
                    if it == 12:
                        emit_h_exp(2, 0)
                    elif it == 13:
                        emit_h_scan(2, 0)
                        emit_h_exp(2, 1)
                        emit_h_exp(2, 2)
                        emit_h_scan(2, 1)
                    elif it == 14:
                        emit_h_exp(2, 3)
                        emit_h_scan(2, 2)
                    elif it == 15:
                        emit_h_scan(2, 3)
                        presc[p + 1] = True
            ev_xnbf_free_t = ffn_ti[0]
            ev_ybf_rd_t = tw

        f0, f1 = cnt["O0"], cnt["O1"]
        ops["gpsimd"].append(lambda eng, v=f0: eng.wait_ge(semh["O0"], v))
        ops["gpsimd"].append(lambda eng, v=f1: eng.wait_ge(semh["O1"], v))

        blk = es.enter_context(nc.Block())

        @blk.sync
        def _(eng):
            for fop in ops["sync"]:
                fop(eng)

        @blk.vector
        def _(eng):
            for fop in ops["vector"]:
                fop(eng)

        @blk.scalar
        def _(eng):
            for fop in ops["scalar"]:
                fop(eng)

        @blk.tensor
        def _(eng):
            for fop in ops["tensor"]:
                fop(eng)

        @blk.gpsimd
        def _(eng):
            for fop in ops["gpsimd"]:
                fop(eng)

    return nc


def _host_prep(L0_attn_Wd, L0_attn_bd, L0_attn_a_log, L0_attn_b_param,
               L0_attn_Wo, L0_attn_bo, L0_ffn_theta, L0_ffn_phi,
               L0_ffn_theta_noise, L0_ffn_phi_noise):
    f = np.float32
    wdt = np.ascontiguousarray(np.asarray(L0_attn_Wd, f).T)
    wot = np.ascontiguousarray(np.asarray(L0_attn_Wo, f).T)
    tht = np.ascontiguousarray(np.asarray(L0_ffn_theta, f))
    thn = np.ascontiguousarray(np.asarray(L0_ffn_theta_noise, f))
    pht = np.ascontiguousarray(np.asarray(L0_ffn_phi, f))
    phn = np.ascontiguousarray(np.asarray(L0_ffn_phi_noise, f))
    a_log = np.asarray(L0_attn_a_log, f)
    b_par = np.asarray(L0_attn_b_param, f)
    bd = np.asarray(L0_attn_bd, f)
    bo = np.asarray(L0_attn_bo, f)
    acol = np.ascontiguousarray(
        a_log.reshape(NDT, P, NST).transpose(1, 0, 2).reshape(P, P))
    cstm = np.zeros((P, 272), f)
    cstm[:, 0:128] = np.eye(P, dtype=f)
    cstm[:, 128:136] = bd.reshape(NDT, P).T
    cstm[:, 136:144] = bo.reshape(NDT, P).T
    cstm[:, 144:272] = np.ascontiguousarray(
        b_par.reshape(NDT, P, NST).transpose(1, 0, 2).reshape(P, P))
    return dict(wdt=wdt, wot=wot, tht=tht, thn=thn, pht=pht, phn=phn,
                cst=cstm, acl=acol)


def kernel(L0_attn_x, L0_attn_Wd, L0_attn_bd, L0_attn_a_log, L0_attn_b_param,
           L0_attn_Wo, L0_attn_bo, L0_ffn_theta, L0_ffn_phi,
           L0_ffn_theta_noise, L0_ffn_phi_noise):
    f = np.float32
    x = np.ascontiguousarray(np.asarray(L0_attn_x, f).reshape(8 * TRE, D))
    shared = _host_prep(L0_attn_Wd, L0_attn_bd, L0_attn_a_log, L0_attn_b_param,
                        L0_attn_Wo, L0_attn_bo, L0_ffn_theta, L0_ffn_phi,
                        L0_ffn_theta_noise, L0_ffn_phi_noise)
    in_maps = []
    for c in range(8):
        if c == 0:
            xi = np.concatenate([np.zeros((W, D), f), x[:TRE]], 0)
        else:
            xi = np.ascontiguousarray(x[c * TRE - W:(c + 1) * TRE])
        m = dict(shared)
        m["xin"] = xi
        in_maps.append(m)
    nc = _build()
    res = run_bass_kernel_spmd(nc, in_maps, core_ids=list(range(8)))
    outs = [np.asarray(r["out"], f) for r in res.results]
    return np.concatenate(outs, 0).reshape(1, 8 * TRE, D)


# version 39
# speedup vs baseline: 1.2047x; 1.2047x over previous
import contextlib
import numpy as np
import concourse.bass as bass
import concourse.mybir as mybir
from concourse.bass_utils import run_bass_kernel_spmd

F32 = mybir.dt.float32
BF16 = mybir.dt.bfloat16
AF = mybir.ActivationFunctionType
ALU = mybir.AluOpType

P = 128
D = 1024
NDT = 8
NST = 16
W = 128
TRE = 2048
TTOT = TRE + W
NPH = 3
TP = 768
NSUB = TP // P
TPL = (640, 768, 768)
NSUBL = (5, 6, 6)
SUBBASE = (0, 5, 11)
PSTART = (0, 640, 1408)
PI_HALF = 1.5707963267948966
ENGS = ("sync", "vector", "scalar", "tensor", "gpsimd")
SEMS = ("LI", "LA", "LB", "LC", "LD", "LE", "LF",
        "LX0", "LX1", "LX2", "LX3", "LX4", "LX5", "LR", "LR1", "LAC", "O0", "O1",
        "LW0", "LW1", "V", "S", "T", "G")


def _build():
    nc = bass.Bass("TRN2", target_bir_lowering=False, debug=False)

    def din(name, shape):
        return nc.dram_tensor(name, shape, F32, kind="ExternalInput").ap()

    xin = din("xin", [TTOT, D])
    wdt = din("wdt", [D, D])
    wot = din("wot", [D, D])
    tht = din("tht", [D, D])
    thn = din("thn", [D, D])
    pht = din("pht", [D, D])
    phn = din("phn", [D, D])
    cst = din("cst", [P, 272])
    acl = din("acl", [P, P])
    out = nc.dram_tensor("out", [TRE, D], F32, kind="ExternalOutput").ap()

    es = contextlib.ExitStack()
    with es:
        def sbuf(n, sh, dt=F32):
            return es.enter_context(nc.sbuf_tensor(n, sh, dt))

        def psum(n):
            return es.enter_context(nc.psum_tensor(n, [P, 512], F32))

        wd_bf = sbuf("wd_bf", [P, 8192], BF16)
        wo_bf = sbuf("wo_bf", [P, 8192], BF16)
        cos_bf = sbuf("cos_bf", [P, 8192], BF16)
        sin_bf = sbuf("sin_bf", [P, 8192], BF16)
        bdg_bf = sbuf("bdg_bf", [P, NDT * NST * P], BF16)
        xn32 = sbuf("xn32", [P, NDT * TP])
        xnbf = sbuf("xnbf", [P, NDT * TP], BF16)
        delta = sbuf("delta", [P, 2 * TP])
        vbuf = sbuf("vbuf", [P, 2 * TP])
        abar = sbuf("abar", [P, 3 * TP])
        grot = sbuf("grot", [P, 4 * TP], BF16)
        ybf = sbuf("ybf", [P, NDT * TP], BF16)
        sfm = sbuf("sfm", [P, 2 * 512])
        xsub = sbuf("xsub", [P, 2 * 1024])
        etmp = sbuf("etmp", [P, 2 * 512])
        r2 = sbuf("r2", [P, 2 * 512])
        i2 = sbuf("i2", [P, 2 * 512])
        osub = sbuf("osub", [P, 2 * 512])
        xres = sbuf("xres", [P, 2 * 512])
        gcar = sbuf("gcar", [P, P], BF16)
        nega = sbuf("nega", [P, P])
        csts = sbuf("csts", [P, 272])
        cvals = sbuf("cvals", [P, 3])
        bn6 = sbuf("bn6", [P, 12 * NSUB])
        bn2 = sbuf("bn2", [P, 2])
        bnA = sbuf("bnA", [P, 2 * NSUB])
        lnv = sbuf("lnv", [P, NSUB])
        rstd = sbuf("rstd", [P, 1])
        negmr = sbuf("negmr", [P, 1])
        rstd6 = sbuf("rstd6", [P, NSUB])
        negmr6 = sbuf("negmr6", [P, NSUB])

        idn_s = csts[:, 0:128]
        bdc_s = csts[:, 128:136]
        boc_s = csts[:, 136:144]
        bcol_s = csts[:, 144:272]

        pt = [psum("pt0"), psum("pt1")]
        pd = [psum("pd0"), psum("pd1")]
        py = [psum("py0"), psum("py1")]
        pe = [psum("pe0"), psum("pe1")]

        semh = {s: es.enter_context(nc.semaphore(s)) for s in SEMS}
        cnt = {s: 0 for s in SEMS}
        wm = {}
        ops = {e: [] for e in ENGS}

        def wait(e, s, v):
            if v is None or v <= 0:
                return
            if wm.get((e, s), 0) >= v:
                return
            wm[(e, s)] = v
            h = semh[s]
            ops[e].append(lambda eng, h=h, v=v: eng.wait_ge(h, v))

        def run(e, fn, inc=None, k=1):
            if inc is None:
                ops[e].append(fn)
                return None
            cnt[inc] += k
            v = cnt[inc]
            h = semh[inc]
            ops[e].append(lambda eng, fn=fn, h=h, k=k: fn(eng).then_inc(h, k))
            return v

        run("sync", lambda e: e.dma_start(out=csts[:], in_=cst), "LI", 16)
        lacl = run("sync", lambda e: e.dma_start(out=nega[:], in_=acl), "LAC", 16)
        run("vector", lambda e: e.memset(gcar[:], 0.0), "V")
        v_ph = run("vector", lambda e: e.memset(cvals[:, 0:1], PI_HALF), "V")
        wait("vector", "V", v_ph)
        run("vector", lambda e: e.tensor_scalar(
            out=cvals[:, 0:1], in0=cvals[:, 0:1],
            scalar1=-4.3711388286737929e-08, scalar2=None, op0=ALU.add), "V")
        run("vector", lambda e: e.memset(cvals[:, 1:2], 1e-5), "V")
        v_cv = run("vector", lambda e: e.memset(cvals[:, 2:3], -1.0), "V")

        wslot = [r2[:, 0:512], r2[:, 512:1024], i2[:, 0:512],
                 i2[:, 512:1024], etmp[:, 0:512], etmp[:, 512:1024]]
        wsem = ["LA", "LB", "LC", "LD", "LE", "LF"]
        wfree = [("G", 0)] * 6
        trig_cmds = []
        trig_add_jobs = []
        wo_cast_jobs = []
        winit_last_g = 0
        winit_vec = [0]
        trig_s_last = [0]
        trig_h1 = [0]
        h0_idx = [kk * 2 for kk in range(8)] + [16 + kk * 2 for kk in range(8)]
        h1_idx = [kk * 2 + 1 for kk in range(8)] + [17 + kk * 2 for kk in range(8)]
        ip = 0

        def wslot_next():
            nonlocal ip
            sl = ip % 6
            ip += 1
            return sl

        p0_xslots = [xsub[:, 0:1024], xsub[:, 1024:2048], delta[:, 0:1024],
                     abar[:, 0:1024], sfm[:, 0:1024], vbuf[:, 0:1024]]
        p0_xsems = ["LX0", "LX1", "LX2", "LX3", "LX4", "LX5"]
        p0_lxev = []
        for ts in range(NSUBL[0]):
            lx = run("sync", lambda e, r=ts * P, ts=ts: e.dma_start(
                out=p0_xslots[ts], in_=xin[r:r + P, :]), p0_xsems[ts], 16)
            p0_lxev.append(lx)
        grp_done = {}
        wds_slots = [vbuf[:, 0:512], vbuf[:, 512:1024], osub[:, 512:1024],
                     xres[:, 0:512], xres[:, 512:1024]]
        wds_sems = ["LX5", "LW0", "LW1", "LR", "LR1"]
        wd_s_jobs = []
        ev_wds_s = [0]
        bdg_ev = [("S", 0)] * NDT

        def emit_diag_s(dtq, nlo, nhi):
            wait("scalar", "LI", 16)
            ev = None
            for nq in range(nlo, nhi):
                t = dtq * NST + nq
                ev = run("scalar", lambda e, t=t: e.activation(
                    out=bdg_bf[:, t * P:(t + 1) * P], in_=idn_s,
                    func=AF.Copy, scale=bcol_s[:, t:t + 1]), "S")
            return ev
        for dtq in range(2):
            bdg_ev[dtq] = ("S", emit_diag_s(dtq, 0, NST))
        jobs = []
        for kk in range(8):
            for h in range(2):
                jobs.append(("cast", wdt[kk * P:(kk + 1) * P, h * 512:(h + 1) * 512],
                             None, wd_bf, kk * 1024 + h * 512))
        for kk in range(8):
            for h in range(2):
                jobs.append(("trig", tht[kk * P:(kk + 1) * P, h * 512:(h + 1) * 512],
                             thn[kk * P:(kk + 1) * P, h * 512:(h + 1) * 512],
                             cos_bf, kk * 1024 + h * 512))
        for kk in range(8):
            for h in range(2):
                jobs.append(("trig", pht[kk * P:(kk + 1) * P, h * 512:(h + 1) * 512],
                             phn[kk * P:(kk + 1) * P, h * 512:(h + 1) * 512],
                             sin_bf, kk * 1024 + h * 512))
        for kk in range(8):
            for h in range(2):
                jobs.append(("wo", wot[kk * P:(kk + 1) * P, h * 512:(h + 1) * 512],
                             None, wo_bf, kk * 1024 + h * 512))

        for kind, src0, src1, dst, off in jobs:
            if kind == "cast":
                piece = off // 512
                if piece >= 11:
                    q = piece - 11
                    la = run("sync", lambda e, s=src0, q=q: e.dma_start(
                        out=wds_slots[q], in_=s), wds_sems[q], 16)
                    wd_s_jobs.append((q, la, dst, off))
                    continue
                sl = wslot_next()
                wait("sync", *wfree[sl])
                la = run("sync", lambda e, s=src0, sl=sl: e.dma_start(
                    out=wslot[sl], in_=s), wsem[sl], 16)
                wait("gpsimd", wsem[sl], la)
                g_ev = run("gpsimd", lambda e, d=dst, o=off, sl=sl: e.tensor_copy(
                    out=d[:, o:o + 512], in_=wslot[sl]), "G")
                wfree[sl] = ("G", g_ev)
                winit_last_g = g_ev
                grp_done[id(dst)] = g_ev
            elif kind == "trig":
                s1_ = wslot_next()
                s2_ = wslot_next()
                wait("sync", *wfree[s1_])
                la = run("sync", lambda e, s=src0, sl=s1_: e.dma_start(
                    out=wslot[sl], in_=s), wsem[s1_], 16)
                wait("sync", *wfree[s2_])
                lb = run("sync", lambda e, s=src1, sl=s2_: e.dma_start(
                    out=wslot[sl], in_=s), wsem[s2_], 16)
                wait("gpsimd", wsem[s1_], la)
                wait("gpsimd", wsem[s2_], lb)
                g_ev = run("gpsimd", lambda e, a=s1_, b=s2_, d=dst, o=off: e.tensor_add(
                    out=d[:, o:o + 512], in0=wslot[a], in1=wslot[b]), "G")
                wfree[s1_] = ("G", g_ev)
                wfree[s2_] = ("G", g_ev)
                use_bias = dst is cos_bf
                trig_cmds.append([("G", g_ev), dst, off, use_bias])
                winit_last_g = g_ev
            else:
                wo_cast_jobs.append((src0, dst, off))

        ev_xsub_free = [0, 0]
        ev_xsub_sp_s = 0
        ev_xT_free = [0, 0]
        ev_xn32_free = ("V", 0)
        ev_xn32_free_s = 0
        ev_xnbf_free_t = 0
        bfree_s = {}
        bfree_v = {}
        ev_delta_rd_v = [("V", 0), ("V", 0)]
        ev_delta_rd_s = [0, 0]
        ev_v_free = [0, 0]
        ev_abar_free = [0, 0, 0]
        ev_g_free_t = [0, 0, 0, 0]
        ev_g_free_v = [("V", 0)] * 4
        ev_sfm_free_t = [0, 0]
        ev_osub_free = [0, 0]
        ev_xres_free = 0
        ev_ybf_rd_t = 0
        ev_et_rd_s = [0, 0]
        ev_et_wr_g = [0, 0]
        d_alt = 0
        prev_vts = 0
        prev_s1 = 0
        prev_s2 = 0
        ev_init_v = 0
        ev_init_s = 0
        ev_p0_slots = {}

        ev_xres_free2 = [("V", 0), ("V", 0)]
        sp_mm = {}
        sp_bank = {}
        sp_done = {}
        vv_done = {}
        yev_mm = {}
        yev_last = 0

        def emit_sp_mm_c(p, dt, c, banks=None, keys=None):
            if banks is None:
                banks = pd
                keys = [("pd", 0), ("pd", 1)]
            w = 384 if c == 0 else TPL[p] - 384
            wait("tensor", "S", a_xnbf_s[p])
            wait("tensor", "S", bfree_s.get(keys[c], 0))
            wait("tensor", "V", bfree_v.get(keys[c], 0))
            gdw = grp_done[id(wd_bf)]
            if isinstance(gdw, list):
                for sm_, evv_ in gdw:
                    wait("tensor", sm_, evv_)
            else:
                wait("tensor", "G", gdw)
            tmm = None
            for k in range(NDT):
                tmm = run("tensor", lambda e, k=k, dt=dt, c=c, w=w, bk=banks[c]: e.matmul(
                    bk[:, 0:w],
                    wd_bf[:, k * 1024 + dt * P:k * 1024 + (dt + 1) * P],
                    xnbf[:, k * TP + c * 384:k * TP + c * 384 + w],
                    start=(k == 0), stop=(k == NDT - 1)),
                    "T" if k == NDT - 1 else None)
            sp_mm[(dt, c)] = tmm
            sp_bank[(dt, c)] = (banks[c], keys[c])

        def emit_sp_mm(p, dt, banks=None, keys=None):
            for c in range(2):
                emit_sp_mm_c(p, dt, c, banks, keys)

        def emit_spE(p, dt, c):
            nonlocal ev_xsub_sp_s
            wait("scalar", "T", sp_mm[(dt, c)])
            wait("scalar", "LI", 16)
            wait("scalar", "S", ev_xsub_sp_s)
            wait("scalar", "V", ev_xsub_free[0])
            if p == 0:
                wait("scalar", "T", ev_p0_slots.get(0, 0))
            w = 384 if c == 0 else TPL[p] - 384
            bk, key = sp_bank[(dt, c)]
            se = run("scalar", lambda e, c=c, dt=dt, w=w, bk=bk: e.activation(
                out=xsub[:, 0:w], in_=bk[:, 0:w], func=AF.Exp,
                bias=bdc_s[:, dt:dt + 1]), "S")
            bfree_s[key] = se
            return se

        def emit_spL(p, dt, c, se):
            nonlocal ev_xsub_sp_s
            par = dt % 2
            wait("scalar", "S", se)
            wait("scalar", ev_delta_rd_v[par][0], ev_delta_rd_v[par][1])
            wait("scalar", "S", ev_delta_rd_s[par])
            if p == 0:
                wait("scalar", "T", ev_p0_slots.get(2, 0))
            for rj in (5, 6, 7):
                wait("scalar", "V", ring_stt[rj])
            w = 384 if c == 0 else TPL[p] - 384
            sl = run("scalar", lambda e, par=par, c=c, w=w: e.activation(
                out=delta[:, par * TP + c * 384:par * TP + c * 384 + w],
                in_=xsub[:, 0:w], func=AF.Ln, bias=1.0), "S")
            ev_xsub_sp_s = sl
            return sl

        def emit_vmult(p, dt):
            par = dt % 2
            use_g = False
            eng = "gpsimd" if use_g else "vector"
            sem = "G" if use_g else "V"
            wait(eng, "S", sp_done[dt])
            wait(eng, "V", ev_v_free[par])
            wait(eng, "V", a_x32_v[p])
            if p == 0:
                wait(eng, "T", ev_p0_slots.get(5, 0))
            if p == 0 and dt < 2:
                wait(eng, "S", ev_wds_s[0])
            for rj in (8, 9, 10):
                wait(eng, "V", ring_stt[rj])
            tp = TPL[p]
            vv = run(eng, lambda e, par=par, dt=dt, tp=tp: e.tensor_tensor(
                out=vbuf[:, par * TP:par * TP + tp],
                in0=delta[:, par * TP:par * TP + tp],
                in1=xn32[:, dt * TP:dt * TP + tp], op=ALU.mult), sem)
            ev_delta_rd_v[par] = (sem, vv)
            vv_done[dt] = (sem, vv)

        a_xnbf_s = {}
        a_x32_v = {}

        def emit_transposes(p, ts, par, vts, src_ap, banks=None, keys=None):
            nonlocal ev_xn32_free, ev_xn32_free_s
            if banks is None:
                banks = [pt[0], pt[1]]
                keys = [("pt", 0), ("pt", 1)]
            last_s = last_v = 0
            for half in range(2):
                wait("tensor", "LI", 16)
                wait("tensor", "V", vts)
                wait("tensor", "V", bfree_v.get(keys[half], 0))
                wait("tensor", "S", bfree_s.get(keys[half], 0))
                tev = None
                for j in range(4):
                    dt = half * 4 + j
                    tev = run("tensor", lambda e, bk=banks[half], j=j, dt=dt, a=src_ap: e.transpose(
                        bk[:, j * P:(j + 1) * P],
                        a[:, dt * P:(dt + 1) * P],
                        idn_s), "T")
                ev_xT_free[par] = tev
                wait("vector", "T", tev)
                wait("vector", ev_xn32_free[0], ev_xn32_free[1])
                wait("vector", "S", ev_xn32_free_s)
                x32ap = xn32[:].rearrange("q (d t) -> q d t", d=NDT)[
                    :, half * 4:(half + 1) * 4, ts * P:(ts + 1) * P]
                vev = run("vector", lambda e, a=x32ap, bk=banks[half]: e.tensor_copy(
                    out=a, in_=bk[:, 0:512]), "V")
                wait("scalar", "V", vev)
                wait("scalar", "T", ev_xnbf_free_t)
                xbfap = xnbf[:].rearrange("q (d t) -> q d t", d=NDT)[
                    :, half * 4:(half + 1) * 4, ts * P:(ts + 1) * P]
                sev = run("scalar", lambda e, a=xbfap, b=x32ap: e.activation(
                    out=a, in_=b, func=AF.Copy), "S")
                bfree_v[keys[half]] = vev
                last_s = sev
                last_v = vev
            return last_s, last_v

        a_last = [0, 0]

        def emit_a_subtile(p, ts, banks=None, keys=None):
            nonlocal prev_vts, prev_s1, prev_s2
            par = ts % 2
            lxs = "LX0" if par == 0 else "LX1"
            wait("sync", "V", ev_xsub_free[par])
            wait("sync", "T", ev_xT_free[par])
            if par == 0:
                wait("sync", "S", ev_xsub_sp_s)
                wait("sync", "V", ring_stt[0])
            else:
                wait("sync", "V", ring_stt[1])
                wait("sync", "V", ring_stt[2])
            lx = run("sync", lambda e, r=(SUBBASE[p] + ts) * P, par=par: e.dma_start(
                out=xsub[:, par * 1024:(par + 1) * 1024], in_=xin[r:r + P, :]),
                lxs, 16)
            wait("vector", lxs, lx)
            wait("vector", "V", prev_vts)
            run("vector", lambda e, par=par: e.bn_stats(
                out=bn6[:, 0:6], in_=xsub[:, par * 1024:par * 1024 + 512]), "V")
            vst = run("vector", lambda e, par=par: e.bn_stats(
                out=bn6[:, 6:12], in_=xsub[:, par * 1024 + 512:par * 1024 + 1024]), "V")
            wait("vector", "V", vst)
            wait("vector", "S", prev_s1)
            vag = run("vector", lambda e: e.bn_aggr(out=bn2[:], in_=bn6[:, 0:12]), "V")
            wait("scalar", "V", vag)
            wait("scalar", "S", prev_s2)
            wait("scalar", "V", prev_vts)
            s1 = run("scalar", lambda e: e.activation(
                out=lnv[:, 0:1], in_=bn2[:, 1:2], func=AF.Ln, bias=cvals[:, 1:2]), "S")
            wait("scalar", "S", s1)
            s2 = run("scalar", lambda e: e.activation(
                out=rstd[:], in_=lnv[:, 0:1], func=AF.Exp, scale=-0.5), "S")
            wait("vector", "S", s2)
            v1 = run("vector", lambda e: e.scalar_tensor_tensor(
                out=negmr[:], in0=bn2[:, 0:1], scalar=cvals[:, 2:3], in1=rstd[:],
                op0=ALU.mult, op1=ALU.mult), "V")
            wait("vector", "V", v1)
            vts = run("vector", lambda e, par=par: e.tensor_scalar(
                out=xsub[:, par * 1024:(par + 1) * 1024],
                in0=xsub[:, par * 1024:(par + 1) * 1024],
                scalar1=rstd[:], scalar2=negmr[:],
                op0=ALU.mult, op1=ALU.add), "V")
            ev_xsub_free[par] = vts
            prev_vts, prev_s1, prev_s2 = vts, s1, s2
            a_last[0], a_last[1] = emit_transposes(
                p, ts, par, vts, xsub[:, par * 1024:(par + 1) * 1024], banks, keys)

        def emit_stage_a(p):
            nonlocal prev_vts, prev_s1, prev_s2, ev_init_v, ev_init_s
            last_s = last_v = 0
            if p == 0:
                xslots = p0_xslots
                xsems = p0_xsems
                lxev = p0_lxev
                nsub0 = NSUBL[0]
                vag = 0
                for ts in range(nsub0):
                    wait("vector", xsems[ts], lxev[ts])
                    run("vector", lambda e, ts=ts, a=xslots[ts]: e.bn_stats(
                        out=bn6[:, ts * 12:ts * 12 + 6], in_=a[:, 0:512]), "V")
                    vst = run("vector", lambda e, ts=ts, a=xslots[ts]: e.bn_stats(
                        out=bn6[:, ts * 12 + 6:ts * 12 + 12], in_=a[:, 512:1024]), "V")
                    wait("vector", "V", vst)
                    vag = run("vector", lambda e, ts=ts: e.bn_aggr(
                        out=bnA[:, ts * 2:ts * 2 + 2],
                        in_=bn6[:, ts * 12:ts * 12 + 12]), "V")
                var_ap = bnA[:].rearrange("q (s t) -> q s t", t=2)[:, 0:nsub0, 1:2]
                mean_ap = bnA[:].rearrange("q (s t) -> q s t", t=2)[:, 0:nsub0, 0:1]
                wait("scalar", "V", vag)
                s1 = run("scalar", lambda e, a=var_ap, w=nsub0: e.activation(
                    out=lnv[:, 0:w], in_=a, func=AF.Ln, bias=cvals[:, 1:2]), "S")
                wait("scalar", "S", s1)
                s2 = run("scalar", lambda e, w=nsub0: e.activation(
                    out=rstd6[:, 0:w], in_=lnv[:, 0:w], func=AF.Exp,
                    scale=-0.5), "S")
                wait("vector", "S", s2)
                v1 = run("vector", lambda e, a=mean_ap, w=nsub0: e.scalar_tensor_tensor(
                    out=negmr6[:, 0:w], in0=a, scalar=cvals[:, 2:3],
                    in1=rstd6[:, 0:w], op0=ALU.mult, op1=ALU.mult), "V")
                prev_vts, prev_s1, prev_s2 = v1, s1, s2
                wait("scalar", "LAC", lacl)
                sa = run("scalar", lambda e: e.activation(
                    out=osub[:, 0:128], in_=nega[:], func=AF.Exp), "S")
                wait("scalar", "S", sa)
                sa = run("scalar", lambda e: e.activation(
                    out=osub[:, 128:256], in_=osub[:, 0:128], func=AF.Ln, bias=1.0), "S")
                wait("vector", "S", sa)
                v_na = run("vector", lambda e: e.tensor_scalar(
                    out=nega[:], in0=osub[:, 128:256], scalar1=1e-4, scalar2=-1.0,
                    op0=ALU.add, op1=ALU.mult), "V")
                ev_init_v = v_na
                ev_init_s = sa
                for ts in range(NSUBL[0]):
                    par = ts % 2
                    wait("vector", "V", v1)
                    wait("vector", "S", s2)
                    vts = run("vector", lambda e, ts=ts, a=xslots[ts]: e.tensor_scalar(
                        out=a, in0=a, scalar1=rstd6[:, ts:ts + 1],
                        scalar2=negmr6[:, ts:ts + 1],
                        op0=ALU.mult, op1=ALU.add), "V")
                    last_s, last_v = emit_transposes(p, ts, par, vts, xslots[ts])
                    ev_p0_slots[ts] = ev_xT_free[par]
                    if ts < 2:
                        ev_xsub_free[ts] = vts
                a_xnbf_s[p] = last_s
                a_x32_v[p] = last_v
            else:
                for ts in range(NSUB):
                    emit_a_subtile(p, ts)
                a_xnbf_s[p] = a_last[0]
                a_x32_v[p] = a_last[1]
                return

        ring_ap = ([xsub[:, 512 + j * 512:1024 + j * 512] for j in range(3)]
                   + [etmp[:, 0:512], etmp[:, 512:1024]]
                   + [delta[:, j * 512:(j + 1) * 512] for j in range(3)]
                   + [vbuf[:, j * 512:(j + 1) * 512] for j in range(3)]
                   + [abar[:, j * 512:(j + 1) * 512] for j in range(3)])
        NRING = len(ring_ap)
        ring_stt = [0] * NRING
        ring_g = [0] * NRING
        gadd_ev = [winit_last_g, winit_last_g]

        a_done_set = [set() for _ in range(NPH + 1)]
        presp = [False] * (NPH + 1)
        presc = [False] * (NPH + 1)
        hoist_sx = {}
        hoist_vs = {}

        def emit_h_exp(pn, n):
            tpn = TPL[pn]
            sl = n % 3
            for rj in {0: (11, 12), 1: (12, 13), 2: ()}[sl]:
                wait("scalar", "V", ring_stt[rj])
            wait("scalar", "V", ev_abar_free[sl])
            wait("scalar", "S", sp_done[0])
            sx = run("scalar", lambda e, sl=sl, n=n, tp=tpn: e.activation(
                out=abar[:, sl * TP:sl * TP + tp],
                in_=delta[:, 0:tp], func=AF.Exp,
                scale=nega[:, n:n + 1]), "S")
            hoist_sx[n] = sx
            ev_delta_rd_s[0] = sx

        def emit_h_scan(pn, n):
            tpn = TPL[pn]
            sl = n % 3
            gsl = n % 4
            wait("vector", "S", hoist_sx[n])
            wait("vector", vv_done[0][0], vv_done[0][1])
            wait("vector", "T", ev_g_free_t[gsl])
            wait("vector", ev_g_free_v[gsl][0], ev_g_free_v[gsl][1])
            vs = run("vector", lambda e, g=gsl, sl=sl, n=n, tp=tpn: e.tensor_tensor_scan(
                out=grot[:, g * TP:g * TP + tp],
                data0=abar[:, sl * TP:sl * TP + tp],
                data1=vbuf[:, 0:tp],
                initial=gcar[:, n:n + 1],
                op0=ALU.mult, op1=ALU.add), "V")
            hoist_vs[n] = vs
            ev_abar_free[sl] = vs
            ev_v_free[0] = vs
            if n == 3:
                wait("vector", "V", vs)
                vc = run("vector", lambda e, tp=tpn: e.tensor_copy(
                    out=gcar[:, 0:4],
                    in_=grot[:].rearrange("q (s t) -> q s t", s=4)[:, :, tp - 1:tp]), "V")
                for s4 in range(4):
                    ev_g_free_v[s4] = ("V", vc)

        for p in range(NPH):
            tp = TPL[p]
            if p == 0:
                emit_stage_a(0)
                for (q_, la_, dst_, off_) in wd_s_jobs:
                    wait("scalar", wds_sems[q_], la_)
                    s_ev = run("scalar", lambda e, d=dst_, o=off_, q=q_: e.activation(
                        out=d[:, o:o + 512], in_=wds_slots[q], func=AF.Copy), "S")
                    ev_wds_s[0] = s_ev
                    if q_ == 3:
                        ev_xres_free2[0] = ("S", s_ev)
                    elif q_ == 4:
                        ev_xres_free2[1] = ("S", s_ev)
                grp_done[id(wd_bf)] = [("G", grp_done[id(wd_bf)]),
                                       ("S", ev_wds_s[0])]
                ev_init_s = max(ev_init_s, ev_wds_s[0])
                for dtq in range(2, 6):
                    bdg_ev[dtq] = ("S", emit_diag_s(dtq, 0, NST))
            else:
                for ts in range(NSUBL[p]):
                    if ts not in a_done_set[p]:
                        emit_a_subtile(p, ts,
                                       [pe[1], py[1]], [("pe", 1), ("py", 1)])
                        a_done_set[p].add(ts)
                a_xnbf_s[p] = a_last[0]
                a_x32_v[p] = a_last[1]
            if not presp[p]:
                emit_sp_mm(p, 0)
                se0 = emit_spE(p, 0, 0)
                sl0 = emit_spL(p, 0, 0, se0)
                se1 = emit_spE(p, 0, 1)
                sp_done[0] = emit_spL(p, 0, 1, se1)
                emit_vmult(p, 0)

            chunks = [(W, 512)] if p == 0 else [(0, 384), (384, 384)]
            nch = NDT * len(chunks)
            wo_pend = []
            cjobs = [(dt, c0, wdth) for (c0, wdth) in chunks
                     for dt in range(NDT)]
            mag_exp = [None] * nch
            ffn_ti = [0]

            def ring_avail_waits(j, eng):
                if j < 3:
                    par = 0 if j == 0 else 1
                    wait(eng, "T", ev_xT_free[par])
                    wait(eng, "V", ev_xsub_free[par])
                elif j < 5:
                    wait(eng, "G", winit_last_g)
                    wait(eng, "V", winit_vec[0])
                elif j < 8:
                    if j == 5:
                        wait(eng, "S", ev_delta_rd_s[0])
                        wait(eng, ev_delta_rd_v[0][0], ev_delta_rd_v[0][1])
                    else:
                        wait(eng, "S", ev_delta_rd_s[0])
                        wait(eng, "S", ev_delta_rd_s[1])
                        wait(eng, ev_delta_rd_v[0][0], ev_delta_rd_v[0][1])
                        wait(eng, ev_delta_rd_v[1][0], ev_delta_rd_v[1][1])
                elif j < 11:
                    if j == 8:
                        wait(eng, "V", ev_v_free[0])
                    else:
                        wait(eng, "V", ev_v_free[0])
                        wait(eng, "V", ev_v_free[1])
                else:
                    wait(eng, "V", ev_abar_free[0])
                    wait(eng, "V", ev_abar_free[1])
                    wait(eng, "V", ev_abar_free[2])

            ffn_mm = {}

            def emit_ffn_mm(i):
                dt, c0, wdth = cjobs[i]
                wait("tensor", "S", trig_s_last[0] if dt < 4 else trig_h1[0])
                wait("tensor", "S", bfree_s.get(("pt", 0), 0))
                wait("tensor", "V", bfree_v.get(("pt", 0), 0))
                tr = None
                for k in range(NDT):
                    tr = run("tensor", lambda e, k=k, dt=dt, c0=c0, w=wdth: e.matmul(
                        pt[0][:, 0:w],
                        cos_bf[:, k * 1024 + dt * P:k * 1024 + (dt + 1) * P],
                        xnbf[:, k * TP + c0:k * TP + c0 + w],
                        start=(k == 0), stop=(k == NDT - 1)),
                        "T" if k == NDT - 1 else None)
                wait("tensor", "S", bfree_s.get(("pt", 1), 0))
                wait("tensor", "V", bfree_v.get(("pt", 1), 0))
                ti = None
                for k in range(NDT):
                    ti = run("tensor", lambda e, k=k, dt=dt, c0=c0, w=wdth: e.matmul(
                        pt[1][:, 0:w],
                        sin_bf[:, k * 1024 + dt * P:k * 1024 + (dt + 1) * P],
                        xnbf[:, k * TP + c0:k * TP + c0 + w],
                        start=(k == 0), stop=(k == NDT - 1)),
                        "T" if k == NDT - 1 else None)
                ffn_ti[0] = ti
                ffn_mm[i] = (tr, ti)

            def emit_ffn_mm_part(i, g):
                dt_, c0, wdth = cjobs[i]
                im = g // 2
                half = g % 2
                src = cos_bf if im == 0 else sin_bf
                bank = pt[im]
                if half == 0:
                    wait("tensor", "S", trig_s_last[0] if dt_ < 4 else trig_h1[0])
                    wait("tensor", "S", bfree_s.get(("pt", im), 0))
                    wait("tensor", "V", bfree_v.get(("pt", im), 0))
                tl = None
                for k in range(half * 4, half * 4 + 4):
                    tl = run("tensor", lambda e, k=k, dt=dt_, c0=c0, w=wdth, b=bank, s=src: e.matmul(
                        b[:, 0:w],
                        s[:, k * 1024 + dt * P:k * 1024 + (dt + 1) * P],
                        xnbf[:, k * TP + c0:k * TP + c0 + w],
                        start=(k == 0), stop=(k == NDT - 1)),
                        "T" if k == NDT - 1 else None)
                if g == 1:
                    ffn_mm[i] = (tl, None)
                elif g == 3:
                    ffn_mm[i] = (ffn_mm[i][0], tl)
                    ffn_ti[0] = tl

            def rslot(i):
                m = {5: 5, 6: 8, 7: 6, 8: 7, 9: 9}
                if i < 5:
                    return i
                if i in m:
                    return m[i]
                if i < NRING:
                    return i
                return i - NRING + 3

            def emit_ffn_chain(i):
                dt, c0, wdth = cjobs[i]
                j = rslot(i)
                li = i % 2
                tr, ti = ffn_mm[i]
                wait("scalar", "T", tr)
                wait("scalar", "G", gadd_ev[li])
                wait("scalar", "V", winit_vec[0])
                sq1 = run("scalar", lambda e, w=wdth, li=li: e.activation(
                    out=r2[:, li * 512:li * 512 + w], in_=pt[0][:, 0:w],
                    func=AF.Square), "S")
                bfree_s[("pt", 0)] = sq1
                wait("scalar", "T", ti)
                sq2 = run("scalar", lambda e, w=wdth, li=li: e.activation(
                    out=i2[:, li * 512:li * 512 + w], in_=pt[1][:, 0:w],
                    func=AF.Square), "S")
                bfree_s[("pt", 1)] = sq2
                wait("gpsimd", "S", sq2)
                wait("gpsimd", "V", ring_stt[j])
                wait("gpsimd", "G", ring_g[j])
                ring_avail_waits(j, "gpsimd")
                ga = run("gpsimd", lambda e, w=wdth, li=li, a=ring_ap[j]: e.tensor_tensor(
                    out=a[:, 0:w], in0=r2[:, li * 512:li * 512 + w],
                    in1=i2[:, li * 512:li * 512 + w], op=ALU.add), "G")
                gadd_ev[li] = ga
                ring_g[j] = ga
                wait("scalar", "G", ga)
                sln = run("scalar", lambda e, w=wdth, a=ring_ap[j]: e.activation(
                    out=a[:, 0:w], in_=a[:, 0:w], func=AF.Ln), "S")
                wait("scalar", "S", sln)
                mag_exp[i] = run("scalar", lambda e, w=wdth, a=ring_ap[j]: e.activation(
                    out=a[:, 0:w], in_=a[:, 0:w], func=AF.Exp, scale=0.5), "S")

            ffn_at = ({6: [0], 7: [1]} if p == 0
                      else {0: [0], 1: [1], 2: [2], 3: [3], 4: [4],
                            5: [], 6: [5], 7: [6]})
            NFC = 2 if p == 0 else 7
            yev_pend = []
            for dt in range(NDT):
                par = dt % 2
                if dt + 1 < NDT:
                    emit_sp_mm_c(p, dt + 1, 0)
                if dt >= 5:
                    for jb in ffn_at.get(dt, []):
                        emit_ffn_mm(jb)
                aux = []
                if dt + 1 < NDT:
                    aux.append(("bmm", dt + 1, 1))
                for (ydt, c_) in yev_pend:
                    aux.append(("yev", ydt, c_))
                yev_pend = []
                if dt + 1 < NDT:
                    aux.append(("spE0", dt + 1, 0))
                    aux.append(("spL0", dt + 1, 0))
                    aux.append(("spE1", dt + 1, 1))
                    aux.append(("spL1", dt + 1, 1))
                if p == 0 and dt < 2:
                    for q_ in range(4):
                        aux.append(("diag", dt + 6, q_))
                if p == 0 and 2 <= dt < 6:
                    aux.append(("sinc", (dt - 2) * 4, 4))
                if p == 0 and dt in (0, 1, 6, 7):
                    q1 = {0: 0, 1: 1, 6: 2, 7: 3}[dt]
                    aux.append(("sin1", q1 * 4, 4))
                if dt < 5:
                    for jb in ffn_at.get(dt, []):
                        for g_ in range(4):
                            aux.append(("fmg", jb, g_))
                assert len(aux) <= NST - 1
                aux_i = 0
                sp_se = {}
                vs = None
                sx = None
                td_c = [0, 0]
                for n in range(NST):
                    sl_ab = n % 3
                    gsl = n % 4
                    hoisted = presc[p] and dt == 0 and n < 4
                    if not hoisted:
                        wait("scalar", "V", ev_abar_free[sl_ab])
                        wait("scalar", "V", ev_init_v)
                        wait("scalar", "S", sp_done[dt])
                        if p == 0 and dt == 0 and n <= 1:
                            wait("scalar", "T", ev_p0_slots.get(3, 0))
                        if n <= 1:
                            for rj in (11, 12, 13):
                                wait("scalar", "V", ring_stt[rj])
                        sx = run("scalar", lambda e, sl=sl_ab, par=par, dt=dt, n=n, tp=tp: e.activation(
                            out=abar[:, sl * TP:sl * TP + tp],
                            in_=delta[:, par * TP:par * TP + tp], func=AF.Exp,
                            scale=nega[:, dt * NST + n:dt * NST + n + 1]), "S")
                    if n >= 1 and aux_i < len(aux):
                        kind, adt, ac = aux[aux_i]
                        aux_i += 1
                        if kind == "yev":
                            wait("scalar", "T", yev_mm[(adt, ac)])
                            wait("scalar", "T", ev_ybf_rd_t)
                            wdth_y = tp - W if p == 0 else 384
                            off_y = W if p == 0 else ac * 384
                            ybank = (pe if adt % 2 else py)[ac]
                            sy = run("scalar", lambda e, dt2=adt, o=off_y, w=wdth_y, bb=ybank: e.activation(
                                out=ybf[:, dt2 * TP + o:dt2 * TP + o + w],
                                in_=bb[:, 0:w], func=AF.Copy), "S")
                            bfree_s[("pe" if adt % 2 else "py", ac)] = sy
                            yev_last = sy
                        elif kind == "spE0":
                            sp_se[0] = emit_spE(p, adt, 0)
                        elif kind == "spL0":
                            emit_spL(p, adt, 0, sp_se[0])
                        elif kind == "spE1":
                            sp_se[1] = emit_spE(p, adt, 1)
                        elif kind == "spL1":
                            sp_done[adt] = emit_spL(p, adt, 1, sp_se[1])
                        elif kind in ("sinc", "sin1"):
                            idxs = (h0_idx if kind == "sinc" else h1_idx)[adt:adt + ac]
                            ttrg = trig_s_last if kind == "sinc" else trig_h1
                            for t_ in idxs:
                                (gsm, g_ev), _, _, _ = trig_cmds[t_]
                                wait("scalar", gsm, g_ev)
                            wait("scalar", "V", v_cv)
                            for t_ in idxs:
                                _, dst_, off_, ub_ = trig_cmds[t_]
                                ttrg[0] = run("scalar", lambda e, d=dst_, o=off_, ub=ub_: e.activation(
                                    out=d[:, o:o + 512], in_=d[:, o:o + 512], func=AF.Sin,
                                    bias=(cvals[:, 0:1] if ub else 0.0), scale=1.0), "S")
                        elif kind == "diag":
                            ev_d = emit_diag_s(adt, ac * 4, ac * 4 + 4)
                            if ac == 3:
                                bdg_ev[adt] = ("S", ev_d)
                        elif kind == "fmg":
                            emit_ffn_mm_part(adt, ac)
                        elif kind == "bmm":
                            emit_sp_mm_c(p, adt, ac)
                    if hoisted:
                        vs = hoist_vs[n]
                    else:
                        wait("vector", "S", sx)
                        wait("vector", vv_done[dt][0], vv_done[dt][1])
                        wait("vector", "T", ev_g_free_t[gsl])
                        wait("vector", ev_g_free_v[gsl][0], ev_g_free_v[gsl][1])
                        vs = run("vector", lambda e, g=gsl, sl=sl_ab, par=par, dt=dt, n=n, tp=tp: e.tensor_tensor_scan(
                            out=grot[:, g * TP:g * TP + tp],
                            data0=abar[:, sl * TP:sl * TP + tp],
                            data1=vbuf[:, par * TP:par * TP + tp],
                            initial=gcar[:, dt * NST + n:dt * NST + n + 1],
                            op0=ALU.mult, op1=ALU.add), "V")
                        ev_abar_free[sl_ab] = vs
                        if n % 4 == 3:
                            g0 = n - 3
                            wait("vector", "V", vs)
                            gs = grot[:].rearrange("q (s t) -> q s t", s=4)[:, :, tp - 1:tp]
                            vc = run("vector", lambda e, a=gs, dt=dt, g0=g0: e.tensor_copy(
                                out=gcar[:, dt * NST + g0:dt * NST + g0 + 4], in_=a), "V")
                            for s4 in range(4):
                                ev_g_free_v[s4] = ("V", vc)
                    if n == 7 and dt + 1 < NDT:
                        emit_vmult(p, dt + 1)
                    ypair = pe if dt % 2 else py
                    ypname = "pe" if dt % 2 else "py"
                    wait("tensor", bdg_ev[dt][0], bdg_ev[dt][1])
                    if p == 0:
                        wait("tensor", "V", vs)
                        if n == 0:
                            wait("tensor", "S", bfree_s.get((ypname, 0), 0))
                            wait("tensor", "V", bfree_v.get((ypname, 0), 0))
                        td = run("tensor", lambda e, g=gsl, dt=dt, n=n, bb=ypair[0], tp=tp: e.matmul(
                            bb[:, 0:tp - W],
                            bdg_bf[:, (dt * NST + n) * P:(dt * NST + n + 1) * P],
                            grot[:, g * TP + W:g * TP + tp],
                            start=(n == 0), stop=(n == NST - 1)), "T")
                        td_c[0] = td
                    else:
                        td = None
                        for c in range(2):
                            wait("tensor", "V", vs)
                            if n == 0:
                                wait("tensor", "S", bfree_s.get((ypname, c), 0))
                                wait("tensor", "V", bfree_v.get((ypname, c), 0))
                            td = run("tensor", lambda e, c=c, g=gsl, dt=dt, n=n, bb=ypair[c]: e.matmul(
                                bb[:, 0:384],
                                bdg_bf[:, (dt * NST + n) * P:(dt * NST + n + 1) * P],
                                grot[:, g * TP + c * 384:g * TP + (c + 1) * 384],
                                start=(n == 0), stop=(n == NST - 1)), "T")
                        td_c[1] = td
                        td_c[0] = td
                    ev_g_free_t[gsl] = td
                ev_v_free[par] = vs
                ev_delta_rd_s[par] = sx
                for c in ([0] if p == 0 else [0, 1]):
                    yev_mm[(dt, c)] = td_c[c]
                    yev_pend.append((dt, c))
                for jb in ffn_at.get(dt, []):
                    emit_ffn_chain(jb)
                if p == 0:
                    for _ in range(3):
                        if wo_cast_jobs:
                            s0_, dst_, off_ = wo_cast_jobs.pop(0)
                            sl = wslot_next()
                            wait("sync", *wfree[sl])
                            la = run("sync", lambda e, s=s0_, sl=sl: e.dma_start(
                                out=wslot[sl], in_=s), wsem[sl], 16)
                            wo_pend.append((sl, la, dst_, off_))
                    while len(wo_pend) > 3 or (not wo_cast_jobs and wo_pend):
                        sl, la, dst_, off_ = wo_pend.pop(0)
                        wait("vector", wsem[sl], la)
                        v_ev = run("vector", lambda e, d=dst_, o=off_, sl=sl: e.tensor_copy(
                            out=d[:, o:o + 512], in_=wslot[sl]), "V")
                        wfree[sl] = ("V", v_ev)
                        winit_vec[0] = v_ev
                        grp_done[id(dst_)] = ("V", v_ev)

            for (ydt, c_) in yev_pend:
                wait("scalar", "T", yev_mm[(ydt, c_)])
                wait("scalar", "T", ev_ybf_rd_t)
                wdth_y = tp - W if p == 0 else 384
                off_y = W if p == 0 else c_ * 384
                ybank = (pe if ydt % 2 else py)[c_]
                sy = run("scalar", lambda e, dt2=ydt, o=off_y, w=wdth_y, bb=ybank: e.activation(
                    out=ybf[:, dt2 * TP + o:dt2 * TP + o + w],
                    in_=bb[:, 0:w], func=AF.Copy), "S")
                bfree_s[("pe" if ydt % 2 else "py", c_)] = sy
                yev_last = sy
            yev_pend = []

            ev_xn32_free = vv_done[NDT - 1]
            ev_xn32_free_s = a_xnbf_s[p]
            if p == 0:
                a_sched = {1: [0], 3: [5], 6: [1], 7: [2, 3]}
            else:
                a_sched = {1: [0], 3: [1], 4: [2],
                           9: [3], 10: [4], 11: [5]}
            for it in range(nch):
                dt, c0, wdth = cjobs[it]
                if NFC + it < nch:
                    emit_ffn_mm(NFC + it)
                if p + 1 < NPH:
                    for ts_ in a_sched.get(it, []):
                        if ts_ not in a_done_set[p + 1]:
                            ev_xnbf_free_t = ffn_ti[0]
                            emit_a_subtile(p + 1, ts_,
                                           [pe[1], py[1]], [("pe", 1), ("py", 1)])
                            a_done_set[p + 1].add(ts_)
                b = it % 2
                tok0 = PSTART[p] + c0
                xrs = it % 2
                lrs = "LR" if xrs == 0 else "LR1"
                wait("sync", ev_xres_free2[xrs][0], ev_xres_free2[xrs][1])
                lr = run("sync", lambda e, t0=tok0, dt=dt, w=wdth, xrs=xrs: e.dma_start(
                    out=xres[:, xrs * 512:xrs * 512 + w],
                    in_=xin[t0:t0 + w, dt * P:(dt + 1) * P].rearrange(
                        "(j q) r -> q j r", q=P)), lrs, 16)
                if NFC + it < nch:
                    emit_ffn_chain(NFC + it)
                gd = grp_done[id(wo_bf)]
                if isinstance(gd, tuple):
                    wait("tensor", gd[0], gd[1])
                else:
                    wait("tensor", "G", gd)
                wait("tensor", "S", yev_last)
                wait("tensor", "S", bfree_s.get(("pd", b), 0))
                wait("tensor", "V", bfree_v.get(("pd", b), 0))
                tw = None
                for k in range(NDT):
                    tw = run("tensor", lambda e, k=k, dt=dt, c0=c0, w=wdth, b=b: e.matmul(
                        pd[b][:, 0:w],
                        wo_bf[:, k * 1024 + dt * P:k * 1024 + (dt + 1) * P],
                        ybf[:, k * TP + c0:k * TP + c0 + w],
                        start=(k == 0), stop=(k == NDT - 1)),
                        "T" if k == NDT - 1 else None)
                slot = it % 2
                jr = rslot(it)
                wait("vector", "T", tw)
                wait("vector", "S", mag_exp[it])
                wait("vector", "LI", 16)
                wait("vector", "T", ev_sfm_free_t[slot])
                if p == 0:
                    wait("vector", "T", ev_p0_slots.get(4, 0))
                vstt = run("vector", lambda e, sl=slot, dt=dt, w=wdth, b=b, a=ring_ap[jr]: e.scalar_tensor_tensor(
                    out=sfm[:, sl * 512:sl * 512 + w], in0=pd[b][:, 0:w],
                    scalar=boc_s[:, dt:dt + 1], in1=a[:, 0:w],
                    op0=ALU.add, op1=ALU.add), "V")
                bfree_v[("pd", b)] = vstt
                ring_stt[jr] = vstt
                tb = pe[0] if it % 2 == 0 else py[0]
                tbn = ("pe", 0) if it % 2 == 0 else ("py", 0)
                nj = wdth // P
                wait("tensor", "V", vstt)
                wait("tensor", "S", bfree_s.get(tbn, 0))
                wait("tensor", "V", bfree_v.get(tbn, 0))
                tt = None
                for jj in range(nj):
                    tt = run("tensor", lambda e, bk=tb, jj=jj, sl=slot: e.transpose(
                        bk[:, jj * P:(jj + 1) * P],
                        sfm[:, sl * 512 + jj * P:sl * 512 + (jj + 1) * P],
                        idn_s), "T")
                ev_sfm_free_t[slot] = tt
                osl = it % 2
                osem = "O0" if osl == 0 else "O1"
                wait("vector", "T", tt)
                wait("vector", lrs, lr)
                wait("vector", osem, ev_osub_free[osl])
                if p == 0 and it < 2:
                    wait("vector", "S", ev_init_s)
                vo = run("vector", lambda e, bk=tb, osl=osl, w=wdth, xrs=xrs: e.tensor_add(
                    out=osub[:, osl * 512:osl * 512 + w],
                    in0=bk[:, 0:w],
                    in1=xres[:, xrs * 512:xrs * 512 + w]), "V")
                ev_xres_free2[xrs] = ("V", vo)
                bfree_v[tbn] = vo
                wait("gpsimd", "V", vo)
                oe = run("gpsimd", lambda e, t0=tok0, dt=dt, w=wdth, osl=osl: e.dma_start(
                    out=out[t0 - W:t0 - W + w, dt * P:(dt + 1) * P].rearrange(
                        "(j q) r -> q j r", q=P),
                    in_=osub[:, osl * 512:osl * 512 + w]), osem, 16)
                ev_osub_free[osl] = oe
                if p == 1 and it == 12:
                    a_xnbf_s[p + 1] = a_last[0]
                    a_x32_v[p + 1] = a_last[1]
                    emit_sp_mm(p + 1, 0, pt, [("pt", 0), ("pt", 1)])
                    se0_ = emit_spE(p + 1, 0, 0)
                    emit_spL(p + 1, 0, 0, se0_)
                    se1_ = emit_spE(p + 1, 0, 1)
                    sp_done[0] = emit_spL(p + 1, 0, 1, se1_)
                    emit_vmult(p + 1, 0)
                    presp[p + 1] = True
                if p == 1:
                    if it == 12:
                        emit_h_exp(2, 0)
                    elif it == 13:
                        emit_h_scan(2, 0)
                        emit_h_exp(2, 1)
                        emit_h_exp(2, 2)
                        emit_h_scan(2, 1)
                    elif it == 14:
                        emit_h_exp(2, 3)
                        emit_h_scan(2, 2)
                    elif it == 15:
                        emit_h_scan(2, 3)
                        presc[p + 1] = True
            ev_xnbf_free_t = ffn_ti[0]
            ev_ybf_rd_t = tw

        f0, f1 = cnt["O0"], cnt["O1"]
        ops["gpsimd"].append(lambda eng, v=f0: eng.wait_ge(semh["O0"], v))
        ops["gpsimd"].append(lambda eng, v=f1: eng.wait_ge(semh["O1"], v))

        blk = es.enter_context(nc.Block())

        @blk.sync
        def _(eng):
            for fop in ops["sync"]:
                fop(eng)

        @blk.vector
        def _(eng):
            for fop in ops["vector"]:
                fop(eng)

        @blk.scalar
        def _(eng):
            for fop in ops["scalar"]:
                fop(eng)

        @blk.tensor
        def _(eng):
            for fop in ops["tensor"]:
                fop(eng)

        @blk.gpsimd
        def _(eng):
            for fop in ops["gpsimd"]:
                fop(eng)

    return nc


def _host_prep(L0_attn_Wd, L0_attn_bd, L0_attn_a_log, L0_attn_b_param,
               L0_attn_Wo, L0_attn_bo, L0_ffn_theta, L0_ffn_phi,
               L0_ffn_theta_noise, L0_ffn_phi_noise):
    f = np.float32
    wdt = np.ascontiguousarray(np.asarray(L0_attn_Wd, f).T)
    wot = np.ascontiguousarray(np.asarray(L0_attn_Wo, f).T)
    tht = np.ascontiguousarray(np.asarray(L0_ffn_theta, f))
    thn = np.ascontiguousarray(np.asarray(L0_ffn_theta_noise, f))
    pht = np.ascontiguousarray(np.asarray(L0_ffn_phi, f))
    phn = np.ascontiguousarray(np.asarray(L0_ffn_phi_noise, f))
    a_log = np.asarray(L0_attn_a_log, f)
    b_par = np.asarray(L0_attn_b_param, f)
    bd = np.asarray(L0_attn_bd, f)
    bo = np.asarray(L0_attn_bo, f)
    acol = np.ascontiguousarray(
        a_log.reshape(NDT, P, NST).transpose(1, 0, 2).reshape(P, P))
    cstm = np.zeros((P, 272), f)
    cstm[:, 0:128] = np.eye(P, dtype=f)
    cstm[:, 128:136] = bd.reshape(NDT, P).T
    cstm[:, 136:144] = bo.reshape(NDT, P).T
    cstm[:, 144:272] = np.ascontiguousarray(
        b_par.reshape(NDT, P, NST).transpose(1, 0, 2).reshape(P, P))
    return dict(wdt=wdt, wot=wot, tht=tht, thn=thn, pht=pht, phn=phn,
                cst=cstm, acl=acol)


def kernel(L0_attn_x, L0_attn_Wd, L0_attn_bd, L0_attn_a_log, L0_attn_b_param,
           L0_attn_Wo, L0_attn_bo, L0_ffn_theta, L0_ffn_phi,
           L0_ffn_theta_noise, L0_ffn_phi_noise):
    f = np.float32
    x = np.ascontiguousarray(np.asarray(L0_attn_x, f).reshape(8 * TRE, D))
    shared = _host_prep(L0_attn_Wd, L0_attn_bd, L0_attn_a_log, L0_attn_b_param,
                        L0_attn_Wo, L0_attn_bo, L0_ffn_theta, L0_ffn_phi,
                        L0_ffn_theta_noise, L0_ffn_phi_noise)
    in_maps = []
    for c in range(8):
        if c == 0:
            xi = np.concatenate([np.zeros((W, D), f), x[:TRE]], 0)
        else:
            xi = np.ascontiguousarray(x[c * TRE - W:(c + 1) * TRE])
        m = dict(shared)
        m["xin"] = xi
        in_maps.append(m)
    nc = _build()
    res = run_bass_kernel_spmd(nc, in_maps, core_ids=list(range(8)))
    outs = [np.asarray(r["out"], f) for r in res.results]
    return np.concatenate(outs, 0).reshape(1, 8 * TRE, D)


# version 40 (final)
# speedup vs baseline: 1.2522x; 1.0394x over previous
import contextlib
import numpy as np
import concourse.bass as bass
import concourse.mybir as mybir
from concourse.bass_utils import run_bass_kernel_spmd

F32 = mybir.dt.float32
BF16 = mybir.dt.bfloat16
AF = mybir.ActivationFunctionType
ALU = mybir.AluOpType

P = 128
D = 1024
NDT = 8
NST = 16
W = 128
TRE = 2048
TTOT = TRE + W
NPH = 3
TP = 768
NSUB = TP // P
TPL = (640, 768, 768)
NSUBL = (5, 6, 6)
SUBBASE = (0, 5, 11)
PSTART = (0, 640, 1408)
PI_HALF = 1.5707963267948966
ENGS = ("sync", "vector", "scalar", "tensor", "gpsimd")
SEMS = ("LI", "LA", "LB", "LC", "LD", "LE", "LF",
        "LX0", "LX1", "LX2", "LX3", "LX4", "LX5", "LR", "LR1", "LAC", "O0", "O1",
        "LW0", "LW1", "V", "S", "T", "G")


def _build():
    nc = bass.Bass("TRN2", target_bir_lowering=False, debug=False)

    def din(name, shape):
        return nc.dram_tensor(name, shape, F32, kind="ExternalInput").ap()

    xin = din("xin", [TTOT, D])
    wdt = din("wdt", [D, D])
    wot = din("wot", [D, D])
    tht = din("tht", [D, D])
    thn = din("thn", [D, D])
    pht = din("pht", [D, D])
    phn = din("phn", [D, D])
    cst = din("cst", [P, 272])
    acl = din("acl", [P, P])
    out = nc.dram_tensor("out", [TRE, D], F32, kind="ExternalOutput").ap()

    es = contextlib.ExitStack()
    with es:
        def sbuf(n, sh, dt=F32):
            return es.enter_context(nc.sbuf_tensor(n, sh, dt))

        def psum(n):
            return es.enter_context(nc.psum_tensor(n, [P, 512], F32))

        wd_bf = sbuf("wd_bf", [P, 8192], BF16)
        wo_bf = sbuf("wo_bf", [P, 8192], BF16)
        cos_bf = sbuf("cos_bf", [P, 8192], BF16)
        sin_bf = sbuf("sin_bf", [P, 8192], BF16)
        bdg_bf = sbuf("bdg_bf", [P, NDT * NST * P], BF16)
        xn32 = sbuf("xn32", [P, NDT * TP])
        xnbf = sbuf("xnbf", [P, NDT * TP], BF16)
        delta = sbuf("delta", [P, 2 * TP])
        vbuf = sbuf("vbuf", [P, 2 * TP])
        abar = sbuf("abar", [P, 3 * TP])
        grot = sbuf("grot", [P, 4 * TP], BF16)
        ybf = sbuf("ybf", [P, NDT * TP], BF16)
        sfm = sbuf("sfm", [P, 2 * 512])
        xsub = sbuf("xsub", [P, 2 * 1024])
        etmp = sbuf("etmp", [P, 2 * 512])
        r2 = sbuf("r2", [P, 2 * 512])
        i2 = sbuf("i2", [P, 2 * 512])
        osub = sbuf("osub", [P, 2 * 512])
        xres = sbuf("xres", [P, 2 * 512])
        gcar = sbuf("gcar", [P, P], BF16)
        nega = sbuf("nega", [P, P])
        csts = sbuf("csts", [P, 272])
        cvals = sbuf("cvals", [P, 3])
        bn6 = sbuf("bn6", [P, 12 * NSUB])
        bn2 = sbuf("bn2", [P, 2])
        bnA = sbuf("bnA", [P, 2 * NSUB])
        lnv = sbuf("lnv", [P, NSUB])
        rstd = sbuf("rstd", [P, 1])
        negmr = sbuf("negmr", [P, 1])
        rstd6 = sbuf("rstd6", [P, NSUB])
        negmr6 = sbuf("negmr6", [P, NSUB])

        idn_s = csts[:, 0:128]
        bdc_s = csts[:, 128:136]
        boc_s = csts[:, 136:144]
        bcol_s = csts[:, 144:272]

        pt = [psum("pt0"), psum("pt1")]
        pd = [psum("pd0"), psum("pd1")]
        py = [psum("py0"), psum("py1")]
        pe = [psum("pe0"), psum("pe1")]

        semh = {s: es.enter_context(nc.semaphore(s)) for s in SEMS}
        cnt = {s: 0 for s in SEMS}
        wm = {}
        ops = {e: [] for e in ENGS}

        def wait(e, s, v):
            if v is None or v <= 0:
                return
            if wm.get((e, s), 0) >= v:
                return
            wm[(e, s)] = v
            h = semh[s]
            ops[e].append(lambda eng, h=h, v=v: eng.wait_ge(h, v))

        def run(e, fn, inc=None, k=1):
            if inc is None:
                ops[e].append(fn)
                return None
            cnt[inc] += k
            v = cnt[inc]
            h = semh[inc]
            ops[e].append(lambda eng, fn=fn, h=h, k=k: fn(eng).then_inc(h, k))
            return v

        run("sync", lambda e: e.dma_start(out=csts[:], in_=cst), "LI", 16)
        lacl = run("sync", lambda e: e.dma_start(out=nega[:], in_=acl), "LAC", 16)
        run("vector", lambda e: e.memset(gcar[:], 0.0), "V")
        v_ph = run("vector", lambda e: e.memset(cvals[:, 0:1], PI_HALF), "V")
        wait("vector", "V", v_ph)
        run("vector", lambda e: e.tensor_scalar(
            out=cvals[:, 0:1], in0=cvals[:, 0:1],
            scalar1=-4.3711388286737929e-08, scalar2=None, op0=ALU.add), "V")
        run("vector", lambda e: e.memset(cvals[:, 1:2], 1e-5), "V")
        v_cv = run("vector", lambda e: e.memset(cvals[:, 2:3], -1.0), "V")

        wslot = [r2[:, 0:512], r2[:, 512:1024], i2[:, 0:512],
                 i2[:, 512:1024], etmp[:, 0:512], etmp[:, 512:1024]]
        wsem = ["LA", "LB", "LC", "LD", "LE", "LF"]
        wfree = [("G", 0)] * 6
        trig_cmds = []
        trig_add_jobs = []
        wo_cast_jobs = []
        winit_last_g = 0
        winit_vec = [0]
        trig_s_last = [0]
        trig_h1 = [0]
        h0_idx = [kk * 2 for kk in range(8)] + [16 + kk * 2 for kk in range(8)]
        h1_idx = [kk * 2 + 1 for kk in range(8)] + [17 + kk * 2 for kk in range(8)]
        ip = 0

        def wslot_next():
            nonlocal ip
            sl = ip % 6
            ip += 1
            return sl

        p0_xslots = [xsub[:, 0:1024], xsub[:, 1024:2048], delta[:, 0:1024],
                     abar[:, 0:1024], sfm[:, 0:1024], vbuf[:, 0:1024]]
        p0_xsems = ["LX0", "LX1", "LX2", "LX3", "LX4", "LX5"]
        p0_lxev = []
        for ts in range(NSUBL[0]):
            lx = run("sync", lambda e, r=ts * P, ts=ts: e.dma_start(
                out=p0_xslots[ts], in_=xin[r:r + P, :]), p0_xsems[ts], 16)
            p0_lxev.append(lx)
        grp_done = {}
        wds_slots = [vbuf[:, 0:512], vbuf[:, 512:1024], osub[:, 512:1024],
                     xres[:, 0:512], xres[:, 512:1024]]
        wds_sems = ["LX5", "LW0", "LW1", "LR", "LR1"]
        wd_s_jobs = []
        ev_wds_s = [0]
        bdg_ev = [("S", 0)] * NDT

        def emit_diag_s(dtq, nlo, nhi):
            wait("scalar", "LI", 16)
            ev = None
            for nq in range(nlo, nhi):
                t = dtq * NST + nq
                ev = run("scalar", lambda e, t=t: e.activation(
                    out=bdg_bf[:, t * P:(t + 1) * P], in_=idn_s,
                    func=AF.Copy, scale=bcol_s[:, t:t + 1]), "S")
            return ev
        for dtq in range(2):
            bdg_ev[dtq] = ("S", emit_diag_s(dtq, 0, NST))
        jobs = []
        for kk in range(8):
            for h in range(2):
                jobs.append(("cast", wdt[kk * P:(kk + 1) * P, h * 512:(h + 1) * 512],
                             None, wd_bf, kk * 1024 + h * 512))
        for kk in range(8):
            for h in range(2):
                jobs.append(("trig", tht[kk * P:(kk + 1) * P, h * 512:(h + 1) * 512],
                             thn[kk * P:(kk + 1) * P, h * 512:(h + 1) * 512],
                             cos_bf, kk * 1024 + h * 512))
        for kk in range(8):
            for h in range(2):
                jobs.append(("trig", pht[kk * P:(kk + 1) * P, h * 512:(h + 1) * 512],
                             phn[kk * P:(kk + 1) * P, h * 512:(h + 1) * 512],
                             sin_bf, kk * 1024 + h * 512))
        for kk in range(8):
            for h in range(2):
                jobs.append(("wo", wot[kk * P:(kk + 1) * P, h * 512:(h + 1) * 512],
                             None, wo_bf, kk * 1024 + h * 512))

        for kind, src0, src1, dst, off in jobs:
            if kind == "cast":
                piece = off // 512
                if piece >= 11:
                    q = piece - 11
                    la = run("sync", lambda e, s=src0, q=q: e.dma_start(
                        out=wds_slots[q], in_=s), wds_sems[q], 16)
                    wd_s_jobs.append((q, la, dst, off))
                    continue
                sl = wslot_next()
                wait("sync", *wfree[sl])
                la = run("sync", lambda e, s=src0, sl=sl: e.dma_start(
                    out=wslot[sl], in_=s), wsem[sl], 16)
                wait("gpsimd", wsem[sl], la)
                g_ev = run("gpsimd", lambda e, d=dst, o=off, sl=sl: e.tensor_copy(
                    out=d[:, o:o + 512], in_=wslot[sl]), "G")
                wfree[sl] = ("G", g_ev)
                winit_last_g = g_ev
                grp_done[id(dst)] = g_ev
            elif kind == "trig":
                s1_ = wslot_next()
                s2_ = wslot_next()
                wait("sync", *wfree[s1_])
                la = run("sync", lambda e, s=src0, sl=s1_: e.dma_start(
                    out=wslot[sl], in_=s), wsem[s1_], 16)
                wait("sync", *wfree[s2_])
                lb = run("sync", lambda e, s=src1, sl=s2_: e.dma_start(
                    out=wslot[sl], in_=s), wsem[s2_], 16)
                wait("gpsimd", wsem[s1_], la)
                wait("gpsimd", wsem[s2_], lb)
                g_ev = run("gpsimd", lambda e, a=s1_, b=s2_, d=dst, o=off: e.tensor_add(
                    out=d[:, o:o + 512], in0=wslot[a], in1=wslot[b]), "G")
                wfree[s1_] = ("G", g_ev)
                wfree[s2_] = ("G", g_ev)
                use_bias = dst is cos_bf
                trig_cmds.append([("G", g_ev), dst, off, use_bias])
                winit_last_g = g_ev
            else:
                wo_cast_jobs.append((src0, dst, off))

        ev_xsub_free = [0, 0]
        ev_xsub_sp_s = 0
        ev_xT_free = [0, 0]
        ev_xn32_free = ("V", 0)
        ev_xn32_free_s = 0
        ev_xnbf_free_t = 0
        bfree_s = {}
        bfree_v = {}
        ev_delta_rd_v = [("V", 0), ("V", 0)]
        ev_delta_rd_s = [0, 0]
        ev_v_free = [0, 0]
        ev_abar_free = [0, 0, 0]
        ev_g_free_t = [0, 0, 0, 0]
        ev_g_free_v = [("V", 0)] * 4
        ev_sfm_free_t = [0, 0]
        ev_osub_free = [0, 0]
        ev_xres_free = 0
        ev_ybf_rd_t = 0
        ev_et_rd_s = [0, 0]
        ev_et_wr_g = [0, 0]
        d_alt = 0
        prev_vts = 0
        prev_s1 = 0
        prev_s2 = 0
        ev_init_v = 0
        ev_init_s = 0
        ev_p0_slots = {}

        ev_xres_free2 = [("V", 0), ("V", 0)]
        sp_mm = {}
        sp_bank = {}
        sp_done = {}
        vv_done = {}
        yev_mm = {}
        yev_last = 0

        def emit_sp_mm_c(p, dt, c, banks=None, keys=None):
            if banks is None:
                banks = pd
                keys = [("pd", 0), ("pd", 1)]
            w = 384 if c == 0 else TPL[p] - 384
            wait("tensor", "S", a_xnbf_s[p])
            wait("tensor", "S", bfree_s.get(keys[c], 0))
            wait("tensor", "V", bfree_v.get(keys[c], 0))
            gdw = grp_done[id(wd_bf)]
            if isinstance(gdw, list):
                for sm_, evv_ in gdw:
                    wait("tensor", sm_, evv_)
            else:
                wait("tensor", "G", gdw)
            tmm = None
            for k in range(NDT):
                tmm = run("tensor", lambda e, k=k, dt=dt, c=c, w=w, bk=banks[c]: e.matmul(
                    bk[:, 0:w],
                    wd_bf[:, k * 1024 + dt * P:k * 1024 + (dt + 1) * P],
                    xnbf[:, k * TP + c * 384:k * TP + c * 384 + w],
                    start=(k == 0), stop=(k == NDT - 1)),
                    "T" if k == NDT - 1 else None)
            sp_mm[(dt, c)] = tmm
            sp_bank[(dt, c)] = (banks[c], keys[c])

        def emit_sp_mm(p, dt, banks=None, keys=None):
            for c in range(2):
                emit_sp_mm_c(p, dt, c, banks, keys)

        def emit_spE(p, dt, c):
            nonlocal ev_xsub_sp_s
            wait("scalar", "T", sp_mm[(dt, c)])
            wait("scalar", "LI", 16)
            wait("scalar", "S", ev_xsub_sp_s)
            wait("scalar", "V", ev_xsub_free[0])
            if p == 0:
                wait("scalar", "T", ev_p0_slots.get(0, 0))
            w = 384 if c == 0 else TPL[p] - 384
            bk, key = sp_bank[(dt, c)]
            se = run("scalar", lambda e, c=c, dt=dt, w=w, bk=bk: e.activation(
                out=xsub[:, 0:w], in_=bk[:, 0:w], func=AF.Exp,
                bias=bdc_s[:, dt:dt + 1]), "S")
            bfree_s[key] = se
            return se

        def emit_spL(p, dt, c, se):
            nonlocal ev_xsub_sp_s
            par = dt % 2
            wait("scalar", "S", se)
            wait("scalar", ev_delta_rd_v[par][0], ev_delta_rd_v[par][1])
            wait("scalar", "S", ev_delta_rd_s[par])
            if p == 0:
                wait("scalar", "T", ev_p0_slots.get(2, 0))
            for rj in (5, 6, 7):
                wait("scalar", "V", ring_stt[rj])
            w = 384 if c == 0 else TPL[p] - 384
            sl = run("scalar", lambda e, par=par, c=c, w=w: e.activation(
                out=delta[:, par * TP + c * 384:par * TP + c * 384 + w],
                in_=xsub[:, 0:w], func=AF.Ln, bias=1.0), "S")
            ev_xsub_sp_s = sl
            return sl

        def emit_vmult(p, dt):
            par = dt % 2
            use_g = False
            eng = "gpsimd" if use_g else "vector"
            sem = "G" if use_g else "V"
            wait(eng, "S", sp_done[dt])
            wait(eng, "V", ev_v_free[par])
            wait(eng, "V", a_x32_v[p])
            if p == 0:
                wait(eng, "T", ev_p0_slots.get(5, 0))
            if p == 0 and dt < 2:
                wait(eng, "S", ev_wds_s[0])
            for rj in (8, 9, 10):
                wait(eng, "V", ring_stt[rj])
            tp = TPL[p]
            vv = run(eng, lambda e, par=par, dt=dt, tp=tp: e.tensor_tensor(
                out=vbuf[:, par * TP:par * TP + tp],
                in0=delta[:, par * TP:par * TP + tp],
                in1=xn32[:, dt * TP:dt * TP + tp], op=ALU.mult), sem)
            ev_delta_rd_v[par] = (sem, vv)
            vv_done[dt] = (sem, vv)

        a_xnbf_s = {}
        a_x32_v = {}

        def emit_transposes(p, ts, par, vts, src_ap, banks=None, keys=None):
            nonlocal ev_xn32_free, ev_xn32_free_s
            if banks is None:
                banks = [pt[0], pt[1]]
                keys = [("pt", 0), ("pt", 1)]
            last_s = last_v = 0
            for half in range(2):
                wait("tensor", "LI", 16)
                wait("tensor", "V", vts)
                wait("tensor", "V", bfree_v.get(keys[half], 0))
                wait("tensor", "S", bfree_s.get(keys[half], 0))
                tev = None
                for j in range(4):
                    dt = half * 4 + j
                    tev = run("tensor", lambda e, bk=banks[half], j=j, dt=dt, a=src_ap: e.transpose(
                        bk[:, j * P:(j + 1) * P],
                        a[:, dt * P:(dt + 1) * P],
                        idn_s), "T")
                ev_xT_free[par] = tev
                wait("vector", "T", tev)
                wait("vector", ev_xn32_free[0], ev_xn32_free[1])
                wait("vector", "S", ev_xn32_free_s)
                x32ap = xn32[:].rearrange("q (d t) -> q d t", d=NDT)[
                    :, half * 4:(half + 1) * 4, ts * P:(ts + 1) * P]
                vev = run("vector", lambda e, a=x32ap, bk=banks[half]: e.tensor_copy(
                    out=a, in_=bk[:, 0:512]), "V")
                wait("scalar", "V", vev)
                wait("scalar", "T", ev_xnbf_free_t)
                xbfap = xnbf[:].rearrange("q (d t) -> q d t", d=NDT)[
                    :, half * 4:(half + 1) * 4, ts * P:(ts + 1) * P]
                sev = run("scalar", lambda e, a=xbfap, b=x32ap: e.activation(
                    out=a, in_=b, func=AF.Copy), "S")
                bfree_v[keys[half]] = vev
                last_s = sev
                last_v = vev
            return last_s, last_v

        a_last = [0, 0]

        def emit_a_subtile(p, ts, banks=None, keys=None):
            nonlocal prev_vts, prev_s1, prev_s2
            par = ts % 2
            lxs = "LX0" if par == 0 else "LX1"
            wait("sync", "V", ev_xsub_free[par])
            wait("sync", "T", ev_xT_free[par])
            if par == 0:
                wait("sync", "S", ev_xsub_sp_s)
                wait("sync", "V", ring_stt[0])
            else:
                wait("sync", "V", ring_stt[1])
                wait("sync", "V", ring_stt[2])
            lx = run("sync", lambda e, r=(SUBBASE[p] + ts) * P, par=par: e.dma_start(
                out=xsub[:, par * 1024:(par + 1) * 1024], in_=xin[r:r + P, :]),
                lxs, 16)
            wait("vector", lxs, lx)
            wait("vector", "V", prev_vts)
            run("vector", lambda e, par=par: e.bn_stats(
                out=bn6[:, 0:6], in_=xsub[:, par * 1024:par * 1024 + 512]), "V")
            vst = run("vector", lambda e, par=par: e.bn_stats(
                out=bn6[:, 6:12], in_=xsub[:, par * 1024 + 512:par * 1024 + 1024]), "V")
            wait("vector", "V", vst)
            wait("vector", "S", prev_s1)
            vag = run("vector", lambda e: e.bn_aggr(out=bn2[:], in_=bn6[:, 0:12]), "V")
            wait("scalar", "V", vag)
            wait("scalar", "S", prev_s2)
            wait("scalar", "V", prev_vts)
            s1 = run("scalar", lambda e: e.activation(
                out=lnv[:, 0:1], in_=bn2[:, 1:2], func=AF.Ln, bias=cvals[:, 1:2]), "S")
            wait("scalar", "S", s1)
            s2 = run("scalar", lambda e: e.activation(
                out=rstd[:], in_=lnv[:, 0:1], func=AF.Exp, scale=-0.5), "S")
            wait("vector", "S", s2)
            v1 = run("vector", lambda e: e.scalar_tensor_tensor(
                out=negmr[:], in0=bn2[:, 0:1], scalar=cvals[:, 2:3], in1=rstd[:],
                op0=ALU.mult, op1=ALU.mult), "V")
            wait("vector", "V", v1)
            vts = run("vector", lambda e, par=par: e.tensor_scalar(
                out=xsub[:, par * 1024:(par + 1) * 1024],
                in0=xsub[:, par * 1024:(par + 1) * 1024],
                scalar1=rstd[:], scalar2=negmr[:],
                op0=ALU.mult, op1=ALU.add), "V")
            ev_xsub_free[par] = vts
            prev_vts, prev_s1, prev_s2 = vts, s1, s2
            a_last[0], a_last[1] = emit_transposes(
                p, ts, par, vts, xsub[:, par * 1024:(par + 1) * 1024], banks, keys)

        def emit_stage_a(p):
            nonlocal prev_vts, prev_s1, prev_s2, ev_init_v, ev_init_s
            last_s = last_v = 0
            if p == 0:
                xslots = p0_xslots
                xsems = p0_xsems
                lxev = p0_lxev
                nsub0 = NSUBL[0]
                vag = 0
                for ts in range(nsub0):
                    wait("vector", xsems[ts], lxev[ts])
                    run("vector", lambda e, ts=ts, a=xslots[ts]: e.bn_stats(
                        out=bn6[:, ts * 12:ts * 12 + 6], in_=a[:, 0:512]), "V")
                    vst = run("vector", lambda e, ts=ts, a=xslots[ts]: e.bn_stats(
                        out=bn6[:, ts * 12 + 6:ts * 12 + 12], in_=a[:, 512:1024]), "V")
                    wait("vector", "V", vst)
                    vag = run("vector", lambda e, ts=ts: e.bn_aggr(
                        out=bnA[:, ts * 2:ts * 2 + 2],
                        in_=bn6[:, ts * 12:ts * 12 + 12]), "V")
                var_ap = bnA[:].rearrange("q (s t) -> q s t", t=2)[:, 0:nsub0, 1:2]
                mean_ap = bnA[:].rearrange("q (s t) -> q s t", t=2)[:, 0:nsub0, 0:1]
                wait("scalar", "V", vag)
                s1 = run("scalar", lambda e, a=var_ap, w=nsub0: e.activation(
                    out=lnv[:, 0:w], in_=a, func=AF.Ln, bias=cvals[:, 1:2]), "S")
                wait("scalar", "S", s1)
                s2 = run("scalar", lambda e, w=nsub0: e.activation(
                    out=rstd6[:, 0:w], in_=lnv[:, 0:w], func=AF.Exp,
                    scale=-0.5), "S")
                wait("vector", "S", s2)
                v1 = run("vector", lambda e, a=mean_ap, w=nsub0: e.scalar_tensor_tensor(
                    out=negmr6[:, 0:w], in0=a, scalar=cvals[:, 2:3],
                    in1=rstd6[:, 0:w], op0=ALU.mult, op1=ALU.mult), "V")
                prev_vts, prev_s1, prev_s2 = v1, s1, s2
                wait("scalar", "LAC", lacl)
                sa = run("scalar", lambda e: e.activation(
                    out=osub[:, 0:128], in_=nega[:], func=AF.Exp), "S")
                wait("scalar", "S", sa)
                sa = run("scalar", lambda e: e.activation(
                    out=osub[:, 128:256], in_=osub[:, 0:128], func=AF.Ln, bias=1.0), "S")
                wait("vector", "S", sa)
                v_na = run("vector", lambda e: e.tensor_scalar(
                    out=nega[:], in0=osub[:, 128:256], scalar1=1e-4, scalar2=-1.0,
                    op0=ALU.add, op1=ALU.mult), "V")
                ev_init_v = v_na
                ev_init_s = sa
                for ts in range(NSUBL[0]):
                    par = ts % 2
                    wait("vector", "V", v1)
                    wait("vector", "S", s2)
                    vts = run("vector", lambda e, ts=ts, a=xslots[ts]: e.tensor_scalar(
                        out=a, in0=a, scalar1=rstd6[:, ts:ts + 1],
                        scalar2=negmr6[:, ts:ts + 1],
                        op0=ALU.mult, op1=ALU.add), "V")
                    last_s, last_v = emit_transposes(p, ts, par, vts, xslots[ts])
                    ev_p0_slots[ts] = ev_xT_free[par]
                    if ts < 2:
                        ev_xsub_free[ts] = vts
                a_xnbf_s[p] = last_s
                a_x32_v[p] = last_v
            else:
                for ts in range(NSUB):
                    emit_a_subtile(p, ts)
                a_xnbf_s[p] = a_last[0]
                a_x32_v[p] = a_last[1]
                return

        ring_ap = ([xsub[:, 512 + j * 512:1024 + j * 512] for j in range(3)]
                   + [etmp[:, 0:512], etmp[:, 512:1024]]
                   + [delta[:, j * 512:(j + 1) * 512] for j in range(3)]
                   + [vbuf[:, j * 512:(j + 1) * 512] for j in range(3)]
                   + [abar[:, j * 512:(j + 1) * 512] for j in range(3)])
        NRING = len(ring_ap)
        ring_stt = [0] * NRING
        ring_g = [0] * NRING
        gadd_ev = [winit_last_g, winit_last_g]

        a_done_set = [set() for _ in range(NPH + 1)]
        presp = [False] * (NPH + 1)
        presc = [False] * (NPH + 1)
        hoist_sx = {}
        hoist_vs = {}

        def emit_h_exp(pn, n):
            tpn = TPL[pn]
            sl = n % 3
            for rj in {0: (11, 12), 1: (12, 13), 2: ()}[sl]:
                wait("scalar", "V", ring_stt[rj])
            wait("scalar", "V", ev_abar_free[sl])
            wait("scalar", "S", sp_done[0])
            sx = run("scalar", lambda e, sl=sl, n=n, tp=tpn: e.activation(
                out=abar[:, sl * TP:sl * TP + tp],
                in_=delta[:, 0:tp], func=AF.Exp,
                scale=nega[:, n:n + 1]), "S")
            hoist_sx[n] = sx
            ev_delta_rd_s[0] = sx

        def emit_h_scan(pn, n):
            tpn = TPL[pn]
            sl = n % 3
            gsl = n % 4
            wait("vector", "S", hoist_sx[n])
            wait("vector", vv_done[0][0], vv_done[0][1])
            wait("vector", "T", ev_g_free_t[gsl])
            wait("vector", ev_g_free_v[gsl][0], ev_g_free_v[gsl][1])
            vs = run("vector", lambda e, g=gsl, sl=sl, n=n, tp=tpn: e.tensor_tensor_scan(
                out=grot[:, g * TP:g * TP + tp],
                data0=abar[:, sl * TP:sl * TP + tp],
                data1=vbuf[:, 0:tp],
                initial=gcar[:, n:n + 1],
                op0=ALU.mult, op1=ALU.add), "V")
            hoist_vs[n] = vs
            ev_abar_free[sl] = vs
            ev_v_free[0] = vs
            if n == 3:
                wait("vector", "V", vs)
                vc = run("vector", lambda e, tp=tpn: e.tensor_copy(
                    out=gcar[:, 0:4],
                    in_=grot[:].rearrange("q (s t) -> q s t", s=4)[:, :, tp - 1:tp]), "V")
                for s4 in range(4):
                    ev_g_free_v[s4] = ("V", vc)

        for p in range(NPH):
            tp = TPL[p]
            if p == 0:
                emit_stage_a(0)
                for (q_, la_, dst_, off_) in wd_s_jobs:
                    wait("scalar", wds_sems[q_], la_)
                    s_ev = run("scalar", lambda e, d=dst_, o=off_, q=q_: e.activation(
                        out=d[:, o:o + 512], in_=wds_slots[q], func=AF.Copy), "S")
                    ev_wds_s[0] = s_ev
                    if q_ == 3:
                        ev_xres_free2[0] = ("S", s_ev)
                    elif q_ == 4:
                        ev_xres_free2[1] = ("S", s_ev)
                grp_done[id(wd_bf)] = [("G", grp_done[id(wd_bf)]),
                                       ("S", ev_wds_s[0])]
                ev_init_s = max(ev_init_s, ev_wds_s[0])
                for dtq in range(2, 6):
                    bdg_ev[dtq] = ("S", emit_diag_s(dtq, 0, NST))
            else:
                for ts in range(NSUBL[p]):
                    if ts not in a_done_set[p]:
                        emit_a_subtile(p, ts,
                                       [pe[1], py[1]], [("pe", 1), ("py", 1)])
                        a_done_set[p].add(ts)
                a_xnbf_s[p] = a_last[0]
                a_x32_v[p] = a_last[1]
            if not presp[p]:
                emit_sp_mm(p, 0)
                se0 = emit_spE(p, 0, 0)
                sl0 = emit_spL(p, 0, 0, se0)
                se1 = emit_spE(p, 0, 1)
                sp_done[0] = emit_spL(p, 0, 1, se1)
                emit_vmult(p, 0)

            chunks = [(W, 512)] if p == 0 else [(0, 384), (384, 384)]
            nch = NDT * len(chunks)
            wo_pend = []
            cjobs = [(dt, c0, wdth) for (c0, wdth) in chunks
                     for dt in range(NDT)]
            mag_exp = [None] * nch
            ffn_ti = [0]

            def ring_avail_waits(j, eng):
                if j < 3:
                    par = 0 if j == 0 else 1
                    wait(eng, "T", ev_xT_free[par])
                    wait(eng, "V", ev_xsub_free[par])
                elif j < 5:
                    wait(eng, "G", winit_last_g)
                    wait(eng, "V", winit_vec[0])
                elif j < 8:
                    if j == 5:
                        wait(eng, "S", ev_delta_rd_s[0])
                        wait(eng, ev_delta_rd_v[0][0], ev_delta_rd_v[0][1])
                    else:
                        wait(eng, "S", ev_delta_rd_s[0])
                        wait(eng, "S", ev_delta_rd_s[1])
                        wait(eng, ev_delta_rd_v[0][0], ev_delta_rd_v[0][1])
                        wait(eng, ev_delta_rd_v[1][0], ev_delta_rd_v[1][1])
                elif j < 11:
                    if j == 8:
                        wait(eng, "V", ev_v_free[0])
                    else:
                        wait(eng, "V", ev_v_free[0])
                        wait(eng, "V", ev_v_free[1])
                else:
                    wait(eng, "V", ev_abar_free[0])
                    wait(eng, "V", ev_abar_free[1])
                    wait(eng, "V", ev_abar_free[2])

            ffn_mm = {}

            def emit_ffn_mm(i):
                dt, c0, wdth = cjobs[i]
                wait("tensor", "S", trig_s_last[0] if dt < 4 else trig_h1[0])
                wait("tensor", "S", bfree_s.get(("pt", 0), 0))
                wait("tensor", "V", bfree_v.get(("pt", 0), 0))
                tr = None
                for k in range(NDT):
                    tr = run("tensor", lambda e, k=k, dt=dt, c0=c0, w=wdth: e.matmul(
                        pt[0][:, 0:w],
                        cos_bf[:, k * 1024 + dt * P:k * 1024 + (dt + 1) * P],
                        xnbf[:, k * TP + c0:k * TP + c0 + w],
                        start=(k == 0), stop=(k == NDT - 1)),
                        "T" if k == NDT - 1 else None)
                wait("tensor", "S", bfree_s.get(("pt", 1), 0))
                wait("tensor", "V", bfree_v.get(("pt", 1), 0))
                ti = None
                for k in range(NDT):
                    ti = run("tensor", lambda e, k=k, dt=dt, c0=c0, w=wdth: e.matmul(
                        pt[1][:, 0:w],
                        sin_bf[:, k * 1024 + dt * P:k * 1024 + (dt + 1) * P],
                        xnbf[:, k * TP + c0:k * TP + c0 + w],
                        start=(k == 0), stop=(k == NDT - 1)),
                        "T" if k == NDT - 1 else None)
                ffn_ti[0] = ti
                ffn_mm[i] = (tr, ti)

            def emit_ffn_mm_part(i, g):
                dt_, c0, wdth = cjobs[i]
                im = g // 2
                half = g % 2
                src = cos_bf if im == 0 else sin_bf
                bank = pt[im]
                if half == 0:
                    wait("tensor", "S", trig_s_last[0] if dt_ < 4 else trig_h1[0])
                    wait("tensor", "S", bfree_s.get(("pt", im), 0))
                    wait("tensor", "V", bfree_v.get(("pt", im), 0))
                tl = None
                for k in range(half * 4, half * 4 + 4):
                    tl = run("tensor", lambda e, k=k, dt=dt_, c0=c0, w=wdth, b=bank, s=src: e.matmul(
                        b[:, 0:w],
                        s[:, k * 1024 + dt * P:k * 1024 + (dt + 1) * P],
                        xnbf[:, k * TP + c0:k * TP + c0 + w],
                        start=(k == 0), stop=(k == NDT - 1)),
                        "T" if k == NDT - 1 else None)
                if g == 1:
                    ffn_mm[i] = (tl, None)
                elif g == 3:
                    ffn_mm[i] = (ffn_mm[i][0], tl)
                    ffn_ti[0] = tl

            def rslot(i):
                m = {5: 5, 6: 8, 7: 6, 8: 7, 9: 9}
                if i < 5:
                    return i
                if i in m:
                    return m[i]
                if i < NRING:
                    return i
                return i - NRING + 3

            def emit_ffn_chain(i):
                dt, c0, wdth = cjobs[i]
                j = rslot(i)
                li = i % 2
                tr, ti = ffn_mm[i]
                wait("scalar", "T", tr)
                wait("scalar", "G", gadd_ev[li])
                wait("scalar", "V", winit_vec[0])
                sq1 = run("scalar", lambda e, w=wdth, li=li: e.activation(
                    out=r2[:, li * 512:li * 512 + w], in_=pt[0][:, 0:w],
                    func=AF.Square), "S")
                bfree_s[("pt", 0)] = sq1
                wait("scalar", "T", ti)
                sq2 = run("scalar", lambda e, w=wdth, li=li: e.activation(
                    out=i2[:, li * 512:li * 512 + w], in_=pt[1][:, 0:w],
                    func=AF.Square), "S")
                bfree_s[("pt", 1)] = sq2
                wait("gpsimd", "S", sq2)
                wait("gpsimd", "V", ring_stt[j])
                wait("gpsimd", "G", ring_g[j])
                ring_avail_waits(j, "gpsimd")
                ga = run("gpsimd", lambda e, w=wdth, li=li, a=ring_ap[j]: e.tensor_tensor(
                    out=a[:, 0:w], in0=r2[:, li * 512:li * 512 + w],
                    in1=i2[:, li * 512:li * 512 + w], op=ALU.add), "G")
                gadd_ev[li] = ga
                ring_g[j] = ga
                wait("scalar", "G", ga)
                sln = run("scalar", lambda e, w=wdth, a=ring_ap[j]: e.activation(
                    out=a[:, 0:w], in_=a[:, 0:w], func=AF.Ln), "S")
                wait("scalar", "S", sln)
                mag_exp[i] = run("scalar", lambda e, w=wdth, a=ring_ap[j]: e.activation(
                    out=a[:, 0:w], in_=a[:, 0:w], func=AF.Exp, scale=0.5), "S")

            ffn_at = ({6: [0], 7: [1]} if p == 0
                      else {0: [0], 1: [1], 2: [2], 3: [3], 4: [4],
                            5: [], 6: [5], 7: [6]})
            NFC = 2 if p == 0 else 7
            yev_pend = []
            for dt in range(NDT):
                par = dt % 2
                if dt + 1 < NDT:
                    emit_sp_mm_c(p, dt + 1, 0)
                if dt >= 5:
                    for jb in ffn_at.get(dt, []):
                        emit_ffn_mm(jb)
                aux = []
                if dt + 1 < NDT:
                    aux.append(("bmm", dt + 1, 1))
                for (ydt, c_) in yev_pend:
                    aux.append(("yev", ydt, c_))
                yev_pend = []
                if dt + 1 < NDT:
                    aux.append(("spE0", dt + 1, 0))
                    aux.append(("spL0", dt + 1, 0))
                    aux.append(("spE1", dt + 1, 1))
                    aux.append(("spL1", dt + 1, 1))
                if p == 0 and dt < 2:
                    for q_ in range(4):
                        aux.append(("diag", dt + 6, q_))
                if p == 0 and 2 <= dt < 6:
                    aux.append(("sinc", (dt - 2) * 4, 4))
                if p == 0 and dt in (0, 1, 6, 7):
                    q1 = {0: 0, 1: 1, 6: 2, 7: 3}[dt]
                    aux.append(("sin1", q1 * 4, 4))
                if dt < 5:
                    for jb in ffn_at.get(dt, []):
                        for g_ in range(4):
                            aux.append(("fmg", jb, g_))
                assert len(aux) <= NST - 1
                aux_i = 0
                sp_se = {}
                vs = None
                sx = None
                td_c = [0, 0]
                for n in range(NST):
                    sl_ab = n % 3
                    gsl = n % 4
                    hoisted = presc[p] and dt == 0 and n < 4
                    if not hoisted:
                        wait("scalar", "V", ev_abar_free[sl_ab])
                        wait("scalar", "V", ev_init_v)
                        wait("scalar", "S", sp_done[dt])
                        if p == 0 and dt == 0 and n <= 1:
                            wait("scalar", "T", ev_p0_slots.get(3, 0))
                        if n <= 1:
                            for rj in (11, 12, 13):
                                wait("scalar", "V", ring_stt[rj])
                        sx = run("scalar", lambda e, sl=sl_ab, par=par, dt=dt, n=n, tp=tp: e.activation(
                            out=abar[:, sl * TP:sl * TP + tp],
                            in_=delta[:, par * TP:par * TP + tp], func=AF.Exp,
                            scale=nega[:, dt * NST + n:dt * NST + n + 1]), "S")
                    if n >= 1 and aux_i < len(aux):
                        kind, adt, ac = aux[aux_i]
                        aux_i += 1
                        if kind == "yev":
                            wait("scalar", "T", yev_mm[(adt, ac)])
                            wait("scalar", "T", ev_ybf_rd_t)
                            wdth_y = tp - W if p == 0 else 384
                            off_y = W if p == 0 else ac * 384
                            ybank = (pe if adt % 2 else py)[ac]
                            sy = run("scalar", lambda e, dt2=adt, o=off_y, w=wdth_y, bb=ybank: e.activation(
                                out=ybf[:, dt2 * TP + o:dt2 * TP + o + w],
                                in_=bb[:, 0:w], func=AF.Copy), "S")
                            bfree_s[("pe" if adt % 2 else "py", ac)] = sy
                            yev_last = sy
                        elif kind == "spE0":
                            sp_se[0] = emit_spE(p, adt, 0)
                        elif kind == "spL0":
                            emit_spL(p, adt, 0, sp_se[0])
                        elif kind == "spE1":
                            sp_se[1] = emit_spE(p, adt, 1)
                        elif kind == "spL1":
                            sp_done[adt] = emit_spL(p, adt, 1, sp_se[1])
                        elif kind in ("sinc", "sin1"):
                            idxs = (h0_idx if kind == "sinc" else h1_idx)[adt:adt + ac]
                            ttrg = trig_s_last if kind == "sinc" else trig_h1
                            for t_ in idxs:
                                (gsm, g_ev), _, _, _ = trig_cmds[t_]
                                wait("scalar", gsm, g_ev)
                            wait("scalar", "V", v_cv)
                            for t_ in idxs:
                                _, dst_, off_, ub_ = trig_cmds[t_]
                                ttrg[0] = run("scalar", lambda e, d=dst_, o=off_, ub=ub_: e.activation(
                                    out=d[:, o:o + 512], in_=d[:, o:o + 512], func=AF.Sin,
                                    bias=(cvals[:, 0:1] if ub else 0.0), scale=1.0), "S")
                        elif kind == "diag":
                            ev_d = emit_diag_s(adt, ac * 4, ac * 4 + 4)
                            if ac == 3:
                                bdg_ev[adt] = ("S", ev_d)
                        elif kind == "fmg":
                            emit_ffn_mm_part(adt, ac)
                        elif kind == "bmm":
                            emit_sp_mm_c(p, adt, ac)
                    if hoisted:
                        vs = hoist_vs[n]
                    else:
                        wait("vector", "S", sx)
                        wait("vector", vv_done[dt][0], vv_done[dt][1])
                        wait("vector", "T", ev_g_free_t[gsl])
                        wait("vector", ev_g_free_v[gsl][0], ev_g_free_v[gsl][1])
                        vs = run("vector", lambda e, g=gsl, sl=sl_ab, par=par, dt=dt, n=n, tp=tp: e.tensor_tensor_scan(
                            out=grot[:, g * TP:g * TP + tp],
                            data0=abar[:, sl * TP:sl * TP + tp],
                            data1=vbuf[:, par * TP:par * TP + tp],
                            initial=gcar[:, dt * NST + n:dt * NST + n + 1],
                            op0=ALU.mult, op1=ALU.add), "V")
                        ev_abar_free[sl_ab] = vs
                        if n % 4 == 3:
                            g0 = n - 3
                            wait("vector", "V", vs)
                            gs = grot[:].rearrange("q (s t) -> q s t", s=4)[:, :, tp - 1:tp]
                            vc = run("vector", lambda e, a=gs, dt=dt, g0=g0: e.tensor_copy(
                                out=gcar[:, dt * NST + g0:dt * NST + g0 + 4], in_=a), "V")
                            for s4 in range(4):
                                ev_g_free_v[s4] = ("V", vc)
                    if n == 7 and dt + 1 < NDT:
                        emit_vmult(p, dt + 1)
                    ypair = pe if dt % 2 else py
                    ypname = "pe" if dt % 2 else "py"
                    wait("tensor", bdg_ev[dt][0], bdg_ev[dt][1])
                    if p == 0:
                        wait("tensor", "V", vs)
                        if n == 0:
                            wait("tensor", "S", bfree_s.get((ypname, 0), 0))
                            wait("tensor", "V", bfree_v.get((ypname, 0), 0))
                        td = run("tensor", lambda e, g=gsl, dt=dt, n=n, bb=ypair[0], tp=tp: e.matmul(
                            bb[:, 0:tp - W],
                            bdg_bf[:, (dt * NST + n) * P:(dt * NST + n + 1) * P],
                            grot[:, g * TP + W:g * TP + tp],
                            start=(n == 0), stop=(n == NST - 1)), "T")
                        td_c[0] = td
                    else:
                        td = None
                        for c in range(2):
                            wait("tensor", "V", vs)
                            if n == 0:
                                wait("tensor", "S", bfree_s.get((ypname, c), 0))
                                wait("tensor", "V", bfree_v.get((ypname, c), 0))
                            td = run("tensor", lambda e, c=c, g=gsl, dt=dt, n=n, bb=ypair[c]: e.matmul(
                                bb[:, 0:384],
                                bdg_bf[:, (dt * NST + n) * P:(dt * NST + n + 1) * P],
                                grot[:, g * TP + c * 384:g * TP + (c + 1) * 384],
                                start=(n == 0), stop=(n == NST - 1)), "T")
                        td_c[1] = td
                        td_c[0] = td
                    ev_g_free_t[gsl] = td
                ev_v_free[par] = vs
                ev_delta_rd_s[par] = sx
                for c in ([0] if p == 0 else [0, 1]):
                    yev_mm[(dt, c)] = td_c[c]
                    yev_pend.append((dt, c))
                for jb in ffn_at.get(dt, []):
                    emit_ffn_chain(jb)
                if p == 0:
                    for _ in range(3):
                        if wo_cast_jobs:
                            s0_, dst_, off_ = wo_cast_jobs.pop(0)
                            sl = wslot_next()
                            wait("sync", *wfree[sl])
                            la = run("sync", lambda e, s=s0_, sl=sl: e.dma_start(
                                out=wslot[sl], in_=s), wsem[sl], 16)
                            wo_pend.append((sl, la, dst_, off_))
                    while len(wo_pend) > 3 or (not wo_cast_jobs and wo_pend):
                        sl, la, dst_, off_ = wo_pend.pop(0)
                        wait("vector", wsem[sl], la)
                        v_ev = run("vector", lambda e, d=dst_, o=off_, sl=sl: e.tensor_copy(
                            out=d[:, o:o + 512], in_=wslot[sl]), "V")
                        wfree[sl] = ("V", v_ev)
                        winit_vec[0] = v_ev
                        grp_done[id(dst_)] = ("V", v_ev)

            for (ydt, c_) in yev_pend:
                wait("scalar", "T", yev_mm[(ydt, c_)])
                wait("scalar", "T", ev_ybf_rd_t)
                wdth_y = tp - W if p == 0 else 384
                off_y = W if p == 0 else c_ * 384
                ybank = (pe if ydt % 2 else py)[c_]
                sy = run("scalar", lambda e, dt2=ydt, o=off_y, w=wdth_y, bb=ybank: e.activation(
                    out=ybf[:, dt2 * TP + o:dt2 * TP + o + w],
                    in_=bb[:, 0:w], func=AF.Copy), "S")
                bfree_s[("pe" if ydt % 2 else "py", c_)] = sy
                yev_last = sy
            yev_pend = []

            ev_xn32_free = vv_done[NDT - 1]
            ev_xn32_free_s = a_xnbf_s[p]
            if p == 0:
                a_sched = {1: [0], 3: [5], 6: [1], 7: [2, 3]}
            else:
                a_sched = {1: [0], 3: [1], 4: [2],
                           9: [3], 10: [4], 11: [5]}
            for it in range(nch):
                dt, c0, wdth = cjobs[it]
                b = it % 2
                gd = grp_done[id(wo_bf)]
                if isinstance(gd, tuple):
                    wait("tensor", gd[0], gd[1])
                else:
                    wait("tensor", "G", gd)
                wait("tensor", "S", yev_last)
                wait("tensor", "S", bfree_s.get(("pd", b), 0))
                wait("tensor", "V", bfree_v.get(("pd", b), 0))
                tw = None
                for k in range(NDT):
                    tw = run("tensor", lambda e, k=k, dt=dt, c0=c0, w=wdth, b=b: e.matmul(
                        pd[b][:, 0:w],
                        wo_bf[:, k * 1024 + dt * P:k * 1024 + (dt + 1) * P],
                        ybf[:, k * TP + c0:k * TP + c0 + w],
                        start=(k == 0), stop=(k == NDT - 1)),
                        "T" if k == NDT - 1 else None)
                if NFC + it < nch:
                    emit_ffn_mm(NFC + it)
                if p + 1 < NPH:
                    for ts_ in a_sched.get(it, []):
                        if ts_ not in a_done_set[p + 1]:
                            ev_xnbf_free_t = ffn_ti[0]
                            emit_a_subtile(p + 1, ts_,
                                           [pe[1], py[1]], [("pe", 1), ("py", 1)])
                            a_done_set[p + 1].add(ts_)
                b = it % 2
                tok0 = PSTART[p] + c0
                xrs = it % 2
                lrs = "LR" if xrs == 0 else "LR1"
                wait("sync", ev_xres_free2[xrs][0], ev_xres_free2[xrs][1])
                lr = run("sync", lambda e, t0=tok0, dt=dt, w=wdth, xrs=xrs: e.dma_start(
                    out=xres[:, xrs * 512:xrs * 512 + w],
                    in_=xin[t0:t0 + w, dt * P:(dt + 1) * P].rearrange(
                        "(j q) r -> q j r", q=P)), lrs, 16)
                if NFC + it < nch:
                    emit_ffn_chain(NFC + it)
                slot = it % 2
                jr = rslot(it)
                wait("vector", "T", tw)
                wait("vector", "S", mag_exp[it])
                wait("vector", "LI", 16)
                wait("vector", "T", ev_sfm_free_t[slot])
                if p == 0:
                    wait("vector", "T", ev_p0_slots.get(4, 0))
                vstt = run("vector", lambda e, sl=slot, dt=dt, w=wdth, b=b, a=ring_ap[jr]: e.scalar_tensor_tensor(
                    out=sfm[:, sl * 512:sl * 512 + w], in0=pd[b][:, 0:w],
                    scalar=boc_s[:, dt:dt + 1], in1=a[:, 0:w],
                    op0=ALU.add, op1=ALU.add), "V")
                bfree_v[("pd", b)] = vstt
                ring_stt[jr] = vstt
                tb = pe[0] if it % 2 == 0 else py[0]
                tbn = ("pe", 0) if it % 2 == 0 else ("py", 0)
                nj = wdth // P
                wait("tensor", "V", vstt)
                wait("tensor", "S", bfree_s.get(tbn, 0))
                wait("tensor", "V", bfree_v.get(tbn, 0))
                tt = None
                for jj in range(nj):
                    tt = run("tensor", lambda e, bk=tb, jj=jj, sl=slot: e.transpose(
                        bk[:, jj * P:(jj + 1) * P],
                        sfm[:, sl * 512 + jj * P:sl * 512 + (jj + 1) * P],
                        idn_s), "T")
                ev_sfm_free_t[slot] = tt
                osl = it % 2
                osem = "O0" if osl == 0 else "O1"
                wait("vector", "T", tt)
                wait("vector", lrs, lr)
                wait("vector", osem, ev_osub_free[osl])
                if p == 0 and it < 2:
                    wait("vector", "S", ev_init_s)
                vo = run("vector", lambda e, bk=tb, osl=osl, w=wdth, xrs=xrs: e.tensor_add(
                    out=osub[:, osl * 512:osl * 512 + w],
                    in0=bk[:, 0:w],
                    in1=xres[:, xrs * 512:xrs * 512 + w]), "V")
                ev_xres_free2[xrs] = ("V", vo)
                bfree_v[tbn] = vo
                wait("gpsimd", "V", vo)
                oe = run("gpsimd", lambda e, t0=tok0, dt=dt, w=wdth, osl=osl: e.dma_start(
                    out=out[t0 - W:t0 - W + w, dt * P:(dt + 1) * P].rearrange(
                        "(j q) r -> q j r", q=P),
                    in_=osub[:, osl * 512:osl * 512 + w]), osem, 16)
                ev_osub_free[osl] = oe
                if p == 1 and it == 12:
                    a_xnbf_s[p + 1] = a_last[0]
                    a_x32_v[p + 1] = a_last[1]
                    emit_sp_mm(p + 1, 0, pt, [("pt", 0), ("pt", 1)])
                    se0_ = emit_spE(p + 1, 0, 0)
                    emit_spL(p + 1, 0, 0, se0_)
                    se1_ = emit_spE(p + 1, 0, 1)
                    sp_done[0] = emit_spL(p + 1, 0, 1, se1_)
                    emit_vmult(p + 1, 0)
                    presp[p + 1] = True
                if p == 1:
                    if it == 12:
                        emit_h_exp(2, 0)
                    elif it == 13:
                        emit_h_scan(2, 0)
                        emit_h_exp(2, 1)
                        emit_h_exp(2, 2)
                        emit_h_scan(2, 1)
                    elif it == 14:
                        emit_h_exp(2, 3)
                        emit_h_scan(2, 2)
                    elif it == 15:
                        emit_h_scan(2, 3)
                        presc[p + 1] = True
            ev_xnbf_free_t = ffn_ti[0]
            ev_ybf_rd_t = tw

        f0, f1 = cnt["O0"], cnt["O1"]
        ops["gpsimd"].append(lambda eng, v=f0: eng.wait_ge(semh["O0"], v))
        ops["gpsimd"].append(lambda eng, v=f1: eng.wait_ge(semh["O1"], v))

        blk = es.enter_context(nc.Block())

        @blk.sync
        def _(eng):
            for fop in ops["sync"]:
                fop(eng)

        @blk.vector
        def _(eng):
            for fop in ops["vector"]:
                fop(eng)

        @blk.scalar
        def _(eng):
            for fop in ops["scalar"]:
                fop(eng)

        @blk.tensor
        def _(eng):
            for fop in ops["tensor"]:
                fop(eng)

        @blk.gpsimd
        def _(eng):
            for fop in ops["gpsimd"]:
                fop(eng)

    return nc


def _host_prep(L0_attn_Wd, L0_attn_bd, L0_attn_a_log, L0_attn_b_param,
               L0_attn_Wo, L0_attn_bo, L0_ffn_theta, L0_ffn_phi,
               L0_ffn_theta_noise, L0_ffn_phi_noise):
    f = np.float32
    wdt = np.ascontiguousarray(np.asarray(L0_attn_Wd, f).T)
    wot = np.ascontiguousarray(np.asarray(L0_attn_Wo, f).T)
    tht = np.ascontiguousarray(np.asarray(L0_ffn_theta, f))
    thn = np.ascontiguousarray(np.asarray(L0_ffn_theta_noise, f))
    pht = np.ascontiguousarray(np.asarray(L0_ffn_phi, f))
    phn = np.ascontiguousarray(np.asarray(L0_ffn_phi_noise, f))
    a_log = np.asarray(L0_attn_a_log, f)
    b_par = np.asarray(L0_attn_b_param, f)
    bd = np.asarray(L0_attn_bd, f)
    bo = np.asarray(L0_attn_bo, f)
    acol = np.ascontiguousarray(
        a_log.reshape(NDT, P, NST).transpose(1, 0, 2).reshape(P, P))
    cstm = np.zeros((P, 272), f)
    cstm[:, 0:128] = np.eye(P, dtype=f)
    cstm[:, 128:136] = bd.reshape(NDT, P).T
    cstm[:, 136:144] = bo.reshape(NDT, P).T
    cstm[:, 144:272] = np.ascontiguousarray(
        b_par.reshape(NDT, P, NST).transpose(1, 0, 2).reshape(P, P))
    return dict(wdt=wdt, wot=wot, tht=tht, thn=thn, pht=pht, phn=phn,
                cst=cstm, acl=acol)


def kernel(L0_attn_x, L0_attn_Wd, L0_attn_bd, L0_attn_a_log, L0_attn_b_param,
           L0_attn_Wo, L0_attn_bo, L0_ffn_theta, L0_ffn_phi,
           L0_ffn_theta_noise, L0_ffn_phi_noise):
    f = np.float32
    x = np.ascontiguousarray(np.asarray(L0_attn_x, f).reshape(8 * TRE, D))
    shared = _host_prep(L0_attn_Wd, L0_attn_bd, L0_attn_a_log, L0_attn_b_param,
                        L0_attn_Wo, L0_attn_bo, L0_ffn_theta, L0_ffn_phi,
                        L0_ffn_theta_noise, L0_ffn_phi_noise)
    in_maps = []
    for c in range(8):
        if c == 0:
            xi = np.concatenate([np.zeros((W, D), f), x[:TRE]], 0)
        else:
            xi = np.ascontiguousarray(x[c * TRE - W:(c + 1) * TRE])
        m = dict(shared)
        m["xin"] = xi
        in_maps.append(m)
    nc = _build()
    res = run_bass_kernel_spmd(nc, in_maps, core_ids=list(range(8)))
    outs = [np.asarray(r["out"], f) for r in res.results]
    return np.concatenate(outs, 0).reshape(1, 8 * TRE, D)
